# version 18
# baseline (speedup 1.0000x reference)
import math
from contextlib import ExitStack

import numpy as np
import ml_dtypes

import concourse.bass as bass
import concourse.mybir as mybir
from concourse.bass_utils import run_bass_kernel_spmd

F32 = mybir.dt.float32
BF16 = mybir.dt.bfloat16
AF = mybir.ActivationFunctionType
ALU = mybir.AluOpType
AX = mybir.AxisListType

D = 1024
T = 4096
PB = 8
NT = 8
TS = 512
NSB = 32
H_IN, DH_IN = 8, 128
H_OUT, DH_OUT = 16, 64
RSQ_IN = 1.0 / math.sqrt(DH_IN)
RSQ_OUT = 1.0 / math.sqrt(DH_OUT)
ENGS = ("tensor", "vector", "scalar", "sync", "gpsimd")

def ffn_edges():
    out = []
    lind = 0
    for c in range(6):
        nsrc = min(c + 1, 5)
        snode = c - nsrc
        eq, ek, ev = lind + nsrc - 1, lind, lind + nsrc // 2
        in_q, in_k, in_v = c - 1, snode, snode + nsrc // 2
        out.append(dict(eq=eq, ek=ek, ev=ev, in_q=in_q, in_k=in_k, in_v=in_v))
        lind += nsrc
    return out

EDGES = ffn_edges()

class KB:
    def __init__(self, nc):
        self.nc = nc
        self.ops = {e: [] for e in ENGS}
        self.sems = {}
        self.cnt = {}
        self.last_wait = {e: {} for e in ENGS}
        self.max_waited = {}
        self.guard = {}
        self.nops = 0

    def add_sem(self, handle, name):
        self.sems[name] = handle
        self.cnt[name] = 0
        self.max_waited[name] = 0

    def bump(self, sem, k):
        self.cnt[sem] += k
        return self.cnt[sem]

    def wait(self, eng, sem, val):
        if val is None or val <= 0:
            return
        assert sem in self.sems, sem
        self.max_waited[sem] = max(self.max_waited[sem], val)
        if self.last_wait[eng].get(sem, 0) >= val:
            return
        self.last_wait[eng][sem] = val
        s = self.sems[sem]
        self.ops[eng].append(lambda e, s=s, v=val: e.wait_ge(s, v))
        self.nops += 1

    def wait_guard(self, eng, res):
        for sem, val in self.guard.get(res, ()):
            self.wait(eng, sem, val)

    def set_guard(self, res, sem, val):
        self.guard[res] = [(sem, val)]

    def add_guard(self, res, sem, val):
        self.guard.setdefault(res, []).append((sem, val))

    def op(self, eng, fn, inc=None):
        self.nops += 1
        if inc is None:
            self.ops[eng].append(fn)
            return None
        sem, k = inc
        v = self.bump(sem, k)
        s = self.sems[sem]
        def wrapped(e, fn=fn, s=s, k=k):
            fn(e).then_inc(s, k)
        self.ops[eng].append(wrapped)
        return v

    def dma(self, q, out, in_, sem, **kw):
        return self.op(
            q, lambda e, out=out, in_=in_, kw=kw: e.dma_start(out=out, in_=in_, **kw),
            inc=(sem, 16))

    def finalize_checks(self):
        for name, mx in self.max_waited.items():
            assert mx <= self.cnt[name], (
                f"sem {name}: waited {mx} > final count {self.cnt[name]} (hang)")

    def emit_all(self, block):
        for eng in ENGS:
            ops = self.ops[eng]
            def runner(e, ops=ops):
                for f in ops:
                    f(e)
            getattr(block, eng)(runner)


class Prog:

    def __init__(self, st, nc):
        self.st = st
        self.nc = nc
        mk = lambda f, *a, **k: st.enter_context(f(*a, **k))
        dt = nc.dram_tensor

        self.x_in = dt("x", [D, T], F32, kind="ExternalInput").ap()
        self.wqkv = dt("wqkv", [D, 3 * D], F32, kind="ExternalInput").ap()
        self.w_o = dt("w_o", [D, D], F32, kind="ExternalInput").ap()
        self.We = dt("We", [20, D, D], F32, kind="ExternalInput").ap()
        self.Wq = dt("Wqm", [6, D, D], F32, kind="ExternalInput").ap()
        self.Wk = dt("Wkm", [6, D, D], F32, kind="ExternalInput").ap()
        self.Wv = dt("Wvm", [6, D, D], F32, kind="ExternalInput").ap()
        self.Wo = dt("Wom", [6, D, D], F32, kind="ExternalInput").ap()
        self.befm = dt("befm", [128, 160], F32, kind="ExternalInput").ap()
        self.gbfm = dt("gbfm", [128, 16], F32, kind="ExternalInput").ap()
        self.sca = dt("sca", [1, 24], F32, kind="ExternalInput").ap()
        self.cosbf = dt("cosbf", [128, T], BF16, kind="ExternalInput").ap()
        self.sinbf = dt("sinbf", [128, T], BF16, kind="ExternalInput").ap()
        self.p2 = dt("p2", [128, 128], BF16, kind="ExternalInput").ap()
        self.y_out = dt("y", [D, T], F32, kind="ExternalOutput").ap()

        self.x_res = dt("x_res", [D, T], F32).ap()
        self.ln_buf = dt("ln_buf", [D, T], BF16).ap()
        self.nodes = [dt(f"node{i}", [D, T], BF16).ap() for i in range(6)]
        self.pq_buf = dt("pq_buf", [D, T], BF16).ap()
        self.pk_buf = dt("pk_buf", [D, T], BF16).ap()
        self.pv_buf = dt("pv_buf", [D, T], BF16).ap()
        self.qh_buf = dt("qh_buf", [D, T], BF16).ap()
        self.kh_buf = dt("kh_buf", [D, T], BF16).ap()
        self.vh_buf = dt("vh_buf", [NSB, 128, D], BF16).ap()
        self.ctx_buf = dt("ctx_buf", [D, T], BF16).ap()
        self.sub_out = dt("sub_out", [D, T], BF16).ap()

        self.a_src = mk(nc.sbuf_tensor, "a_src", [128, 16384], BF16)
        self.a_e = mk(nc.sbuf_tensor, "a_e", [128, 16384], BF16)
        self.a_q = [mk(nc.sbuf_tensor, f"a_q{i}", [128, T], BF16) for i in range(2)]
        self.a_k = [mk(nc.sbuf_tensor, f"a_k{i}", [128, T], BF16) for i in range(2)]
        self.a_v = [mk(nc.sbuf_tensor, f"a_v{i}", [128, T], BF16) for i in range(2)]
        self.a_ctx = mk(nc.sbuf_tensor, "a_ctx", [128, T], BF16)
        self.a_x = mk(nc.sbuf_tensor, "a_x", [128, T], F32)
        self.a_y = mk(nc.sbuf_tensor, "a_y", [128, T], F32)
        self.wb = [mk(nc.sbuf_tensor, f"wb{i}", [128, 8192], BF16) for i in range(2)]
        self.a_st = [mk(nc.sbuf_tensor, f"ast{i}", [128, 2048], BF16) for i in range(2)]
        self.a_f = [mk(nc.sbuf_tensor, f"af{i}", [128, 512], F32) for i in range(5)]
        self.c_be = mk(nc.sbuf_tensor, "c_be", [128, 160], F32)
        self.c_gb = mk(nc.sbuf_tensor, "c_gb", [128, 16], F32)
        self.c_sca_row = mk(nc.sbuf_tensor, "c_sca_row", [1, 24], F32)
        self.c_scb = mk(nc.sbuf_tensor, "c_scb", [128, 24], F32)
        self.c_p2 = mk(nc.sbuf_tensor, "c_p2", [128, 128], BF16)
        self.ones_bf = mk(nc.sbuf_tensor, "ones_bf", [128, 128], BF16)
        self.oneD_bf = mk(nc.sbuf_tensor, "oneD_bf", [128, 128], BF16)
        self.ones1f = mk(nc.sbuf_tensor, "ones1f", [1, 128], F32)
        self.ones_f = mk(nc.sbuf_tensor, "ones_f", [128, 128], F32)
        self.c_stat = mk(nc.sbuf_tensor, "c_stat", [128, 32], F32)
        self.c_bias = mk(nc.sbuf_tensor, "c_bias", [128, 8], F32)
        self.c_eps = mk(nc.sbuf_tensor, "c_eps", [128, 2], F32)

        self.p_rot = mk(nc.psum_tensor, "p_rot", [128, 2048], F32)
        self.p_ctx = mk(nc.psum_tensor, "p_ctx", [128, 512], F32)
        self.p_z = mk(nc.psum_tensor, "p_z", [128, 512], F32)
        self.p_ab = mk(nc.psum_tensor, "p_ab", [128, 1024], F32)

        self.kb = KB(nc)
        for name in ("ld", "st", "wg", "t", "v", "s"):
            self.kb.add_sem(mk(nc.semaphore, f"sem_{name}"), name)

        self.wb_guard = [None, None]
        self.SC_EW = lambda c, j: c * 3 + j
        self.SC_AW = lambda c: 18 + c

    def fm(self, dram_ap):
        return dram_ap.rearrange("(b p) t -> p b t", p=128)

    def ew_ap(self, c, j):
        col = self.SC_EW(c, j)
        return self.c_scb[:, col:col + 1]

    def aw_ap(self, c):
        col = self.SC_AW(c)
        return self.c_scb[:, col:col + 1]

    def emit_consts(self):
        kb = self.kb
        kb.dma("sync", self.c_be[:, :], self.befm, "ld")
        kb.dma("sync", self.c_gb[:, :], self.gbfm, "ld")
        kb.dma("sync", self.c_sca_row[:, :], self.sca, "ld")
        l6 = kb.dma("sync", self.c_p2[:, :], self.p2, "ld")
        kb.op("vector", lambda e: e.memset(self.ones_bf[:, :], 1.0))
        kb.op("vector", lambda e: e.memset(self.oneD_bf[:, :], 1.0 / 1024.0))
        kb.op("vector", lambda e: e.memset(self.c_eps[:, 0:1], 1e-5))
        kb.op("vector", lambda e: e.memset(self.c_eps[:, 1:2], 1e-6))
        kb.op("vector", lambda e: e.memset(self.ones_f[:, :], 1.0))
        v1 = kb.op("vector", lambda e: e.memset(self.ones1f[:, :], 1.0), inc=("v", 1))
        kb.wait("tensor", "ld", l6)
        kb.wait("tensor", "v", v1)
        t = None
        for j in range(24):
            t = kb.op("tensor",
                      lambda e, j=j: e.matmul(self.p_ab[:, j:j + 1], self.ones1f[:1, :],
                                              self.c_sca_row[:1, j:j + 1],
                                              start=True, stop=True),
                      inc=("t", 1) if j == 23 else None)
        kb.wait("vector", "t", t)
        v2 = kb.op("vector", lambda e: e.tensor_copy(self.c_scb[:, :], self.p_ab[:, 0:24]),
                   inc=("v", 1))
        kb.set_guard("p_ab0", "v", v2)
        kb.set_guard("p_ab1", "v", v2)

        for eng in ("vector", "scalar", "tensor"):
            kb.wait(eng, "ld", l6)
        kb.wait("scalar", "v", v1)
        kb.wait("tensor", "v", v1)

    def load_weight(self, w_ap, wi, dout=D):
        kb = self.kb
        if self.wb_guard[wi] is not None:
            sem, val = self.wb_guard[wi]
            kb.wait("gpsimd", sem, val)
        r = w_ap.rearrange("(k p) n -> p k n", p=128)
        out = self.wb[wi][:, 0:8 * dout].rearrange("p (k n) -> p k n", k=PB)
        return kb.dma("gpsimd", out, r, "wg")

    def proj(self, src, w_ap, dst, wi, scale_ap=None, be_idx=None, evac=None,
             aux_load=None, dout_blocks=PB):
        kb = self.kb
        wld = self.load_weight(w_ap, wi, dout=dout_blocks * 128)
        srcv = self.fm(src)
        dstv = self.fm(dst)
        sname, dname = self._nm(src), self._nm(dst)
        ldq = {}

        def issue_load(q):
            half = q % 2
            kb.wait_guard("sync", f"a_srcH{half}")
            if f"w:{sname}:q{q}" in kb.guard:
                kb.wait_guard("sync", f"w:{sname}:q{q}")
            else:
                kb.wait_guard("sync", f"w:{sname}")
            ldq[q] = kb.dma("sync",
                            self.a_src[:, half * 8192:(half + 1) * 8192]
                            .rearrange("p (k n) -> p k n", k=PB),
                            srcv[:, :, q * 1024:(q + 1) * 1024], "ld")
            kb.set_guard(f"r:{sname}", "ld", ldq[q])

        issue_load(0)
        if aux_load is not None:
            aux_load(0)
        bank = 0
        for q in range(4):
            half = q % 2
            base = half * 8192
            if q + 1 < 4:
                issue_load(q + 1)
                if aux_load is not None:
                    aux_load(q + 1)
            first_mm = True
            for mb in range(dout_blocks):
                si = mb % 2
                for tl in range(2):
                    bank = (bank + 1) % 4
                    kb.wait_guard("tensor", f"p_rot{bank}")
                    if first_mm:
                        kb.wait("tensor", "ld", ldq[q])
                        if q == 0:
                            kb.wait("tensor", "wg", wld)
                        first_mm = False
                    pslice = self.p_rot[:, bank * 512:(bank + 1) * 512]
                    tmil = None
                    dw = dout_blocks * 128
                    for k in range(PB):
                        tmil = kb.op(
                            "tensor",
                            lambda e, k=k, mb=mb, tl=tl, pslice=pslice, dw=dw, base=base:
                            e.matmul(
                                pslice,
                                self.wb[wi][:, k * dw + mb * 128:k * dw + mb * 128 + 128],
                                self.a_src[:, base + k * 1024 + tl * 512:
                                            base + k * 1024 + tl * 512 + 512],
                                start=(k == 0), stop=(k == PB - 1)),
                            inc=("t", 1) if k == PB - 1 else None)
                    kb.wait("vector", "t", tmil)
                    kb.wait_guard("vector", f"a_st{si}")
                    stage = self.a_st[si][:, tl * 512:(tl + 1) * 512]
                    if evac is not None:
                        vmil = evac(kb, mb, q * 2 + tl, pslice, stage)
                    elif be_idx is not None:
                        be_ap = self.c_be[:, be_idx * 8 + mb:be_idx * 8 + mb + 1]
                        vmil = kb.op("vector",
                                     lambda e, pslice=pslice, stage=stage, be_ap=be_ap:
                                     e.tensor_scalar(stage, pslice, be_ap, None, ALU.add),
                                     inc=("v", 1))
                    elif scale_ap is not None:
                        vmil = kb.op("vector",
                                     lambda e, pslice=pslice, stage=stage, s1=scale_ap:
                                     e.tensor_scalar(stage, pslice, s1, None, ALU.mult),
                                     inc=("v", 1))
                    else:
                        vmil = kb.op("vector",
                                     lambda e, pslice=pslice, stage=stage:
                                     e.tensor_copy(stage, pslice),
                                     inc=("v", 1))
                    kb.set_guard(f"p_rot{bank}", "v", vmil)
                kb.wait("scalar", "v", vmil)
                kb.wait_guard("scalar", f"r:{dname}")
                smil = kb.dma("scalar", dstv[:, mb, q * 1024:(q + 1) * 1024],
                              self.a_st[si][:, 0:1024], "st")
                kb.set_guard(f"a_st{si}", "st", smil)
            kb.set_guard(f"a_srcH{half}", "t", tmil)
            kb.set_guard(f"w:{dname}:q{q}", "st", smil)
        kb.set_guard(f"w:{dname}", "st", smil)
        self.wb_guard[wi] = ("t", self.kb.cnt["t"])
        return smil

    def proj_tm(self, src, w_ap, wi, scale_ap):
        kb = self.kb
        wld = self.load_weight(w_ap, wi)
        srcv = self.fm(src)
        sname = self._nm(src)
        vhv = self.vh_buf.rearrange("j p d -> p j d")
        ldq = {}

        def issue_load(q):
            half = q % 2
            kb.wait_guard("sync", f"a_srcH{half}")
            if f"w:{sname}:q{q}" in kb.guard:
                kb.wait_guard("sync", f"w:{sname}:q{q}")
            else:
                kb.wait_guard("sync", f"w:{sname}")
            ldq[q] = kb.dma("sync",
                            self.a_src[:, half * 8192:(half + 1) * 8192]
                            .rearrange("p (k n) -> p k n", k=PB),
                            srcv[:, :, q * 1024:(q + 1) * 1024], "ld")
            kb.set_guard(f"r:{sname}", "ld", ldq[q])

        issue_load(0)
        bank = 0
        for q in range(4):
            half = q % 2
            base = half * 8192
            if q + 1 < 4:
                issue_load(q + 1)
            first_mm = True
            for sbl in range(8):
                sb = q * 8 + sbl
                si = sbl % 2
                for dt_ in (0, 1):
                    bank = (bank + 1) % 4
                    kb.wait_guard("tensor", f"p_rot{bank}")
                    if first_mm:
                        kb.wait("tensor", "ld", ldq[q])
                        if q == 0:
                            kb.wait("tensor", "wg", wld)
                        first_mm = False
                    pslice = self.p_rot[:, bank * 512:(bank + 1) * 512]
                    tmil = None
                    for k in range(PB):
                        tmil = kb.op(
                            "tensor",
                            lambda e, k=k, sbl=sbl, dt_=dt_, pslice=pslice, base=base:
                            e.matmul(
                                pslice,
                                self.a_src[:, base + k * 1024 + sbl * 128:
                                            base + k * 1024 + sbl * 128 + 128],
                                self.wb[wi][:, k * 1024 + dt_ * 512:k * 1024 + dt_ * 512 + 512],
                                start=(k == 0), stop=(k == PB - 1)),
                            inc=("t", 1) if k == PB - 1 else None)
                    kb.wait("vector", "t", tmil)
                    kb.wait_guard("vector", f"a_st{si}")
                    stage = self.a_st[si][:, dt_ * 512:(dt_ + 1) * 512]
                    if scale_ap is not None:
                        vmil = kb.op("vector",
                                     lambda e, pslice=pslice, stage=stage, s1=scale_ap:
                                     e.tensor_scalar(stage, pslice, s1, None, ALU.mult),
                                     inc=("v", 1))
                    else:
                        vmil = kb.op("vector",
                                     lambda e, pslice=pslice, stage=stage:
                                     e.tensor_copy(stage, pslice), inc=("v", 1))
                    kb.set_guard(f"p_rot{bank}", "v", vmil)
                kb.wait("scalar", "v", vmil)
                kb.wait_guard("scalar", "r:vh_buf")
                smil = kb.dma("scalar", vhv[:, sb:sb + 1, :],
                              self.a_st[si][:, 0:1024].rearrange("p (j d) -> p j d", j=1),
                              "st")
                kb.set_guard(f"a_st{si}", "st", smil)
            kb.set_guard(f"a_srcH{half}", "t", tmil)
        kb.set_guard("w:vh_buf", "st", smil)
        self.wb_guard[wi] = ("t", self.kb.cnt["t"])
        return smil

    def load_head(self, h, dh, slot, qh_src, kh_src):
        kb = self.kb
        kb.wait_guard("sync", f"a_q{slot}")
        kb.wait_guard("sync", f"w:{self._nm(qh_src)}")
        lq = kb.dma("sync", self.a_q[slot][0:dh, :], qh_src[h * dh:(h + 1) * dh, :], "ld")
        kb.set_guard(f"r:{self._nm(qh_src)}", "ld", lq)
        kb.wait_guard("sync", f"a_k{slot}")
        kb.wait_guard("sync", f"w:{self._nm(kh_src)}")
        lk = kb.dma("sync", self.a_k[slot][0:dh, :], kh_src[h * dh:(h + 1) * dh, :], "ld")
        kb.set_guard(f"r:{self._nm(kh_src)}", "ld", lk)
        kb.wait_guard("sync", f"a_v{slot}")
        kb.wait_guard("sync", "w:vh_buf")
        vhv = self.vh_buf[:, :, h * dh:(h + 1) * dh].rearrange("j p d -> p j d")
        av = self.a_v[slot][:, 0:NSB * dh].rearrange("p (j d) -> p j d", d=dh)
        lv = None
        for part in range(4):
            lv = kb.dma("sync", av[:, part * 8:(part + 1) * 8, :],
                        vhv[:, part * 8:(part + 1) * 8, :], "ld")
        kb.set_guard("r:vh_buf", "ld", lv)
        return lq, lk, lv

    def stats_head(self, h, dh, slot, rsq, lq, lk):
        kb = self.kb
        cs = slot * 16
        tprev = None
        for side, (buf, lbuf) in enumerate(((self.a_q[slot], lq), (self.a_k[slot], lk))):
            for chunk in range(2):
                si = chunk % 2
                kb.wait("vector", "ld", lbuf)
                kb.wait_guard("vector", f"a_st{si}")
                vsq = kb.op("vector", lambda e, buf=buf, chunk=chunk, si=si:
                            e.tensor_tensor(self.a_st[si][0:dh, 0:2048],
                                            buf[0:dh, chunk * 2048:(chunk + 1) * 2048],
                                            buf[0:dh, chunk * 2048:(chunk + 1) * 2048],
                                            ALU.mult), inc=("v", 1))
                for ti in range(4):
                    tt = chunk * 4 + ti
                    hp = tt % 2
                    kb.wait_guard("tensor", f"p_ab{hp}")
                    if ti == 0:
                        kb.wait("tensor", "v", vsq)
                    tprev = kb.op("tensor", lambda e, ti=ti, hp=hp, si=si: e.matmul(
                        self.p_ab[:, hp * 512:(hp + 1) * 512],
                        self.ones_bf[0:dh, :],
                        self.a_st[si][0:dh, ti * 512:(ti + 1) * 512],
                        start=True, stop=True), inc=("t", 1))
                    kb.wait("vector", "t", tprev)
                    vm = kb.op("vector", lambda e, tt=tt, hp=hp, side=side, cs=cs:
                               e.reduce_max(self.c_stat[:, cs + side * 8 + tt:
                                                        cs + side * 8 + tt + 1],
                                            self.p_ab[:, hp * 512:(hp + 1) * 512],
                                            axis=AX.X), inc=("v", 1))
                    kb.set_guard(f"p_ab{hp}", "v", vm)
                kb.set_guard(f"a_st{si}", "t", tprev)
        b0, b1, b2, b3 = slot * 4, slot * 4 + 1, slot * 4 + 2, slot * 4 + 3
        kb.wait("vector", "v", kb.cnt["v"])
        kb.op("vector", lambda e, cs=cs, b0=b0: e.reduce_max(
            self.c_bias[:, b0:b0 + 1], self.c_stat[:, cs:cs + 8], axis=AX.X))
        r2 = kb.op("vector", lambda e, cs=cs, b1=b1: e.reduce_max(
            self.c_bias[:, b1:b1 + 1], self.c_stat[:, cs + 8:cs + 16], axis=AX.X),
            inc=("v", 1))
        kb.wait("vector", "v", r2)
        vmil = kb.op("vector", lambda e, b0=b0, b1=b1, b3=b3: e.tensor_tensor(
            self.c_bias[:, b3:b3 + 1], self.c_bias[:, b0:b0 + 1],
            self.c_bias[:, b1:b1 + 1], ALU.mult), inc=("v", 1))
        kb.wait("scalar", "v", vmil)
        kb.op("scalar", lambda e, b1=b1, b3=b3: e.activation(
            self.c_bias[:, b1:b1 + 1], self.c_bias[:, b3:b3 + 1], AF.Ln))
        smil = kb.op("scalar", lambda e, b0=b0, b1=b1: e.activation(
            self.c_bias[:, b0:b0 + 1], self.c_bias[:, b1:b1 + 1], AF.Exp,
            scale=0.5), inc=("s", 1))
        kb.wait("vector", "s", smil)
        vmil = kb.op("vector", lambda e, rsq=rsq, b0=b0, b2=b2: e.tensor_scalar(
            self.c_bias[:, b2:b2 + 1], self.c_bias[:, b0:b0 + 1], -rsq, None,
            ALU.mult), inc=("v", 1))
        return vmil

    def attn_main(self, h, dh, slot, ctx_dst, rsq, bias_ready, lv):
        kb = self.kb
        NZT = 12
        aq, ak, av_ = self.a_q[slot], self.a_k[slot], self.a_v[slot]
        bias_ap = self.c_bias[:, slot * 4 + 2:slot * 4 + 3]
        ctx_done = None
        vm = None
        if not hasattr(self, "_pair_ctr"):
            self._pair_ctr = 0
        for tt in range(NT):
            exp_done = {}
            e_guard_new = {}
            first_sc = True
            vadd = None
            pend = None
            norm_due = [self.pending_norm is not None]

            pairs = [(("p_rot0", "p_rot1"), self.p_rot[:, 0:1024]),
                     (("p_rot2", "p_rot3"), self.p_rot[:, 1024:2048]),
                     (("p_ab0", "p_ab1"), self.p_ab[:, 0:1024])]

            def emit_exp(glo, ghi, tms, pap):
                for tm in tms:
                    kb.wait("scalar", "t", tm)
                waited = []
                missing = False
                for g in range(glo, ghi + 1):
                    prevm = self.e_guard.get(g)
                    if prevm is None:
                        missing = True
                    else:
                        waited += prevm
                if missing:
                    kb.wait_guard("scalar", "a_e")
                for sem, val in waited:
                    kb.wait("scalar", sem, val)
                if glo == 0:
                    kb.wait("scalar", "v", bias_ready)
                sm = kb.op("scalar", lambda e, glo=glo, ghi=ghi, pap=pap:
                           e.activation(
                               self.a_e[:, glo * 1024:(ghi + 1) * 1024],
                               pap, AF.Exp,
                               bias=bias_ap, scale=rsq), inc=("s", 1))
                for g in range(glo, ghi + 1):
                    exp_done[g] = sm
                if norm_due[0]:
                    self.pending_norm()
                    self.pending_norm = None
                    norm_due[0] = False
                return sm

            def emit_scores(g, tt=tt):
                nonlocal first_sc, pend
                self._pair_ctr = (self._pair_ctr + 1) % 3
                ctr = self._pair_ctr
                bnames, pap = pairs[ctr]
                for b in bnames:
                    kb.wait_guard("tensor", b)
                if first_sc:
                    kb.wait("tensor", "ld", lv)
                    first_sc = False
                tm = None
                for jj in range(2):
                    j = 2 * g + jj
                    tm = kb.op("tensor", lambda e, j=j, jj=jj, pap=pap, tt=tt: e.matmul(
                        pap[:, jj * 512:(jj + 1) * 512],
                        ak[0:dh, j * 128:(j + 1) * 128],
                        aq[0:dh, tt * 512:(tt + 1) * 512],
                        start=True, stop=True), inc=("t", 1) if jj == 1 else None)
                if ctr == 0:
                    pend = (g, tm)
                    return
                if ctr == 1 and pend is not None:
                    g0, tm0 = pend
                    pend = None
                    sm = emit_exp(g0, g, [tm0, tm], self.p_rot[:, 0:2048])
                    for b in ("p_rot0", "p_rot1", "p_rot2", "p_rot3"):
                        kb.set_guard(b, "s", sm)
                else:
                    sm = emit_exp(g, g, [tm], pap)
                    for b in bnames:
                        kb.set_guard(b, "s", sm)

            def flush_pend():
                nonlocal pend
                if pend is not None:
                    g0, tm0 = pend
                    pend = None
                    sm = emit_exp(g0, g0, [tm0], self.p_rot[:, 0:1024])
                    for b in ("p_rot0", "p_rot1"):
                        kb.set_guard(b, "s", sm)

            def emit_ctxz(g, tt=tt):
                nonlocal vadd
                kb.wait("tensor", "s", exp_done[g])
                if g == 0:
                    kb.wait_guard("tensor", "p_ctx")
                tm = None
                for jj in range(2):
                    j = 2 * g + jj
                    tm = kb.op("tensor", lambda e, j=j: e.matmul(
                        self.p_ctx[0:dh, :],
                        av_[:, j * dh:(j + 1) * dh],
                        self.a_e[:, j * 512:(j + 1) * 512],
                        start=(j == 0), stop=(j == NSB - 1),
                        skip_group_check=True),
                        inc=("t", 1) if (jj == 1 and g < NZT) else None)
                    if g >= NZT:
                        if g == NZT and jj == 0:
                            kb.wait_guard("tensor", "p_z")
                        tm = kb.op("tensor", lambda e, j=j, g=g, jj=jj: e.matmul(
                            self.p_z[:, :],
                            self.ones_bf[:, :],
                            self.a_e[:, j * 512:(j + 1) * 512],
                            start=(g == NZT and jj == 0), stop=False,
                            skip_group_check=True), inc=("t", 1) if jj == 1 else None)
                e_guard_new[g] = [("t", tm)]
                if g < NZT:
                    kb.wait("vector", "s", exp_done[g])
                    kb.wait("vector", "t", tm)
                    if g == 0:
                        kb.wait_guard("vector", "a_fz")
                        vadd = kb.op("vector", lambda e: e.tensor_tensor(
                            self.a_f[3][:, :], self.a_e[:, 0:512],
                            self.a_e[:, 512:1024], ALU.add), inc=("v", 1))
                    else:
                        for jj in range(2):
                            j = 2 * g + jj
                            vadd = kb.op("vector", lambda e, j=j: e.tensor_tensor(
                                self.a_f[3][:, :], self.a_f[3][:, :],
                                self.a_e[:, j * 512:(j + 1) * 512], ALU.add),
                                inc=("v", 1) if jj == 1 else None)
                    e_guard_new[g] = [("v", vadd)]
                return tm

            emit_scores(0)
            emit_scores(1)
            emit_scores(2)
            for g in range(3, 16):
                emit_ctxz(g - 3)
                emit_scores(g)
            flush_pend()
            emit_ctxz(13)
            emit_ctxz(14)
            emit_ctxz(15)
            kb.wait("tensor", "v", vadd)
            ctx_done = kb.op("tensor", lambda e: e.matmul(
                self.p_z[:, :], self.ones_f[:, :], self.a_f[3][:, :],
                start=False, stop=True, skip_group_check=True), inc=("t", 1))
            kb.set_guard("a_fz", "t", ctx_done)
            self.e_guard = dict(e_guard_new)

            def norm(tt=tt, ctx_done=ctx_done, dh=dh, h=h, slot=slot,
                     ctx_dst=ctx_dst):
                kb.wait("scalar", "t", ctx_done)
                kb.wait_guard("scalar", "a_f2")
                sm = kb.op("scalar", lambda e: e.activation(
                    self.a_f[0][:, :], self.p_z[:, :], AF.Ln), inc=("s", 1))
                kb.set_guard("p_z", "s", sm)
                sm = kb.op("scalar", lambda e: e.activation(
                    self.a_f[1][:, :], self.a_f[0][:, :], AF.Exp, scale=-1.0),
                    inc=("s", 1))
                kb.wait("vector", "t", ctx_done)
                kb.wait_guard("vector", "a_f3")
                cvm = kb.op("vector", lambda e: e.tensor_copy(
                    self.a_f[2][0:dh, :], self.p_ctx[0:dh, :]), inc=("v", 1))
                kb.set_guard("p_ctx", "v", cvm)
                kb.wait("vector", "s", sm)
                kb.wait_guard("vector", "a_ctx")
                vm = kb.op("vector", lambda e, tt=tt: e.tensor_tensor(
                    self.a_ctx[0:dh, tt * 512:(tt + 1) * 512],
                    self.a_f[2][0:dh, :], self.a_f[1][0:dh, :], ALU.mult),
                    inc=("v", 1))
                kb.set_guard("a_f2", "v", vm)
                kb.set_guard("a_f3", "v", vm)
                if tt == NT - 1:
                    kb.wait("gpsimd", "v", vm)
                    kb.wait_guard("gpsimd", f"r:{self._nm(ctx_dst)}")
                    st = kb.dma("gpsimd", ctx_dst[h * dh:(h + 1) * dh, :],
                                self.a_ctx[0:dh, :], "st")
                    kb.set_guard("a_ctx", "st", st)
                    kb.set_guard(f"w:{self._nm(ctx_dst)}", "st", st)
            self.pending_norm = norm

        kb.set_guard(f"a_q{slot}", "t", ctx_done)
        kb.set_guard(f"a_k{slot}", "t", ctx_done)
        kb.set_guard(f"a_v{slot}", "t", ctx_done)
        return ctx_done

    def attn_all(self, nheads, dh, qh_src, kh_src, ctx_dst, rsq):
        self.e_guard = {}
        self.pending_norm = None
        pend = {}
        pend[0] = self.load_head(0, dh, 0, qh_src, kh_src)
        bias = {0: self.stats_head(0, dh, 0, rsq, pend[0][0], pend[0][1])}
        last = None
        for h in range(nheads):
            if h + 1 < nheads:
                slot = (h + 1) % 2
                pend[h + 1] = self.load_head(h + 1, dh, slot, qh_src, kh_src)
                bias[h + 1] = self.stats_head(h + 1, dh, slot, rsq,
                                              pend[h + 1][0], pend[h + 1][1])
            last = self.attn_main(h, dh, h % 2, ctx_dst, rsq, bias[h], pend[h][2])
        self.pending_norm()
        self.pending_norm = None
        self.kb.set_guard("a_e", "t", last)

    def boundary(self, x_src, sub_src, coef, x_dst, ln_dst, eps, affine=False,
                 src_is_bf=False):
        kb = self.kb
        xsv = self.fm(x_src) if x_src is not None else None
        subv = self.fm(sub_src) if sub_src is not None else None
        xdv = self.fm(x_dst) if x_dst is not None else None
        lnv = self.fm(ln_dst) if ln_dst is not None else None
        actxf = self.a_ctx[:, :].bitcast(F32)
        astf = self.a_st[0][:, :].bitcast(F32)
        stats = [
            [self.a_f[i][:, :] for i in range(5)],
            [actxf[:, i * 512:(i + 1) * 512] for i in range(4)]
            + [astf[:, 0:512]],
        ]
        pp_name = [("p_ab0", "p_ab1"), ("p_rot0", "p_rot1")]
        pp_ap = [self.p_ab, self.p_rot[:, 0:1024]]
        if eps is not None:
            eps_ap = self.c_eps[:, 0:1] if eps == 1e-5 else self.c_eps[:, 1:2]
        for tt in range(NT):
            p = tt % 2
            xs = self.a_x if p == 0 else self.a_y
            xs_name = "bx0" if p == 0 else "bx1"
            mu, musq, var, rstd, tmp = stats[p]
            if x_src is not None:
                kb.wait_guard("sync", xs_name)
                if f"w:{self._nm(x_src)}:q{tt // 2}" in kb.guard:
                    kb.wait_guard("sync", f"w:{self._nm(x_src)}:q{tt // 2}")
                else:
                    kb.wait_guard("sync", f"w:{self._nm(x_src)}")
                lx = kb.dma("sync", xs[:, 0:T],
                            xsv[:, :, tt * 512:(tt + 1) * 512], "ld")
                kb.set_guard(f"r:{self._nm(x_src)}", "ld", lx)
            if sub_src is not None:
                kb.wait_guard("sync", f"a_q{p}")
                if f"w:{self._nm(sub_src)}:q{tt // 2}" in kb.guard:
                    kb.wait_guard("sync", f"w:{self._nm(sub_src)}:q{tt // 2}")
                else:
                    kb.wait_guard("sync", f"w:{self._nm(sub_src)}")
                ls = kb.dma("sync", self.a_q[p][:, 0:T],
                            subv[:, :, tt * 512:(tt + 1) * 512], "ld")
                kb.set_guard(f"r:{self._nm(sub_src)}", "ld", ls)
            res_mil = None
            if sub_src is not None and not src_is_bf:
                kb.wait("vector", "ld", lx)
                kb.wait("vector", "ld", ls)
                for b in range(PB):
                    res_mil = kb.op("vector", lambda e, b=b, xs=xs, p=p:
                                    e.scalar_tensor_tensor(
                        xs[:, b * 512:(b + 1) * 512],
                        self.a_q[p][:, b * 512:(b + 1) * 512], coef,
                        xs[:, b * 512:(b + 1) * 512],
                        ALU.mult, ALU.add), inc=("v", 1) if b == PB - 1 else None)
                kb.set_guard(f"a_q{p}", "v", res_mil)
            elif src_is_bf:
                kb.wait("scalar", "ld", ls)
            else:
                kb.wait("vector", "ld", lx)
            if x_dst is not None:
                kb.wait("scalar", "v", res_mil)
                kb.wait_guard("scalar", f"r:{self._nm(x_dst)}")
                stx = kb.dma("scalar", xdv[:, :, tt * 512:(tt + 1) * 512],
                             xs[:, 0:T], "st")
                kb.set_guard(f"w:{self._nm(x_dst)}", "st", stx)
                kb.add_guard(xs_name, "st", stx)
            if ln_dst is None:
                if x_src is not None:
                    kb.add_guard(xs_name, "v", res_mil)
                continue
            if src_is_bf:
                xbf = self.a_q[p]
            else:
                xbf = self.a_k[p]
                kb.wait("scalar", "v", res_mil)
                kb.wait("scalar", "ld", lx)
                kb.wait_guard("scalar", f"a_k{p}")
                for b in range(PB):
                    kb.op("scalar", lambda e, b=b, xs=xs, xbf=xbf: e.activation(
                        xbf[:, b * 512:(b + 1) * 512],
                        xs[:, b * 512:(b + 1) * 512], AF.Copy))
            kb.wait_guard("scalar", f"a_v{p}")
            sm_sq = None
            for b in range(PB):
                sm_sq = kb.op("scalar", lambda e, b=b, p=p, xbf=xbf: e.activation(
                    self.a_v[p][:, b * 512:(b + 1) * 512],
                    xbf[:, b * 512:(b + 1) * 512], AF.Square),
                    inc=("s", 1) if b == PB - 1 else None)
            kb.wait("tensor", "s", sm_sq)
            kb.wait_guard("tensor", pp_name[p][0])
            kb.wait_guard("tensor", pp_name[p][1])
            pp = pp_ap[p]
            for k in range(PB):
                kb.op("tensor", lambda e, k=k, pp=pp, xbf=xbf: e.matmul(
                    pp[:, 0:512], self.oneD_bf[:, :],
                    xbf[:, k * 512:(k + 1) * 512],
                    start=(k == 0), stop=(k == PB - 1), skip_group_check=True))
            tm = None
            for k in range(PB):
                tm = kb.op("tensor", lambda e, k=k, pp=pp, p=p: e.matmul(
                    pp[:, 512:1024], self.oneD_bf[:, :],
                    self.a_v[p][:, k * 512:(k + 1) * 512],
                    start=(k == 0), stop=(k == PB - 1), skip_group_check=True),
                    inc=("t", 1) if k == PB - 1 else None)
            kb.set_guard(f"a_v{p}", "t", tm)
            if src_is_bf:
                kb.set_guard(f"a_q{p}", "t", tm)
            else:
                kb.set_guard(f"a_k{p}", "t", tm)
            kb.wait("vector", "t", tm)
            kb.wait_guard("vector", f"a_fvar{p}")
            if p == 1:
                kb.wait_guard("vector", "a_ctx")
                kb.wait_guard("vector", "a_st0")
            kb.op("vector", lambda e, mu=mu, pp=pp: e.tensor_copy(mu, pp[:, 0:512]))
            kb.op("vector", lambda e, mu=mu, musq=musq: e.tensor_tensor(
                musq, mu, mu, ALU.mult))
            vm1 = kb.op("vector", lambda e, var=var, musq=musq, pp=pp: e.tensor_tensor(
                var, pp[:, 512:1024], musq, ALU.subtract), inc=("v", 1))
            kb.set_guard(pp_name[p][0], "v", vm1)
            kb.set_guard(pp_name[p][1], "v", vm1)
            kb.wait("scalar", "v", vm1)
            kb.op("scalar", lambda e, var=var, rstd=rstd, eps_ap=eps_ap: e.activation(
                rstd, var, AF.Ln, bias=eps_ap))
            sm2 = kb.op("scalar", lambda e, rstd=rstd: e.activation(
                rstd, rstd, AF.Exp, scale=-0.5), inc=("s", 1))
            kb.add_guard(f"a_fvar{p}", "s", sm2)
            kb.wait("vector", "s", sm2)
            kb.wait_guard("vector", f"a_ebH{p}")
            vm2 = None
            for b in range(PB):
                src_b = (xbf if src_is_bf else xs)[:, b * 512:(b + 1) * 512]
                kb.op("vector", lambda e, src_b=src_b, tmp=tmp, mu=mu: e.tensor_tensor(
                    tmp, src_b, mu, ALU.subtract))
                if affine:
                    kb.op("vector", lambda e, tmp=tmp, rstd=rstd: e.tensor_tensor(
                        tmp, tmp, rstd, ALU.mult))
                    vm2 = kb.op("vector", lambda e, b=b, p=p, tmp=tmp: e.tensor_scalar(
                        self.a_e[:, p * 8192 + b * 512:p * 8192 + (b + 1) * 512],
                        tmp, self.c_gb[:, b:b + 1], self.c_gb[:, 8 + b:9 + b],
                        ALU.mult, ALU.add), inc=("v", 1) if b == PB - 1 else None)
                else:
                    vm2 = kb.op("vector", lambda e, b=b, p=p, tmp=tmp, rstd=rstd:
                                e.tensor_tensor(
                        self.a_e[:, p * 8192 + b * 512:p * 8192 + (b + 1) * 512],
                        tmp, rstd, ALU.mult), inc=("v", 1) if b == PB - 1 else None)
            kb.set_guard(f"a_fvar{p}", "v", vm2)
            if p == 1:
                kb.set_guard("a_ctx", "v", vm2)
                kb.set_guard("a_st0", "v", vm2)
            if x_src is not None:
                kb.set_guard(xs_name, "v", vm2)
                if x_dst is not None:
                    kb.add_guard(xs_name, "st", stx)
            if src_is_bf:
                kb.add_guard(f"a_q{p}", "v", vm2)
            kb.wait("scalar", "v", vm2)
            kb.wait_guard("scalar", f"r:{self._nm(ln_dst)}")
            stl = kb.dma("scalar", lnv[:, :, tt * 512:(tt + 1) * 512],
                         self.a_e[:, p * 8192:p * 8192 + 4096], "st")
            kb.set_guard(f"w:{self._nm(ln_dst)}", "st", stl)
            if tt % 2 == 1:
                kb.set_guard(f"w:{self._nm(ln_dst)}:q{tt // 2}", "st", stl)
            kb.set_guard(f"a_ebH{p}", "st", stl)
        if ln_dst is not None:
            kb.set_guard("a_e", "st", kb.cnt["st"])
            kb.wait_guard("sync", "a_ebH0")
            kb.wait_guard("sync", "a_ebH1")


    def _nm(self, ap):
        return ap.tensor.name if hasattr(ap, "tensor") else str(id(ap))

    def rope(self, buf):
        kb = self.kb
        bufv = self.fm(buf)
        kb.wait_guard("scalar", "a_e")
        lc = kb.dma("scalar", self.a_e[:, 0:T], self.cosbf, "st")
        lc = kb.dma("scalar", self.a_e[:, T:2 * T], self.sinbf, "st")
        for pb in range(PB):
            kb.wait_guard("scalar", "a_q0")
            kb.wait_guard("scalar", f"w:{self._nm(buf)}")
            lq = kb.dma("scalar", self.a_q[0][:, 0:T], bufv[:, pb, :], "st")
            vm_all = None
            for tt in range(NT):
                hp = tt % 2
                kb.wait_guard("tensor", f"p_ab{hp}")
                if tt == 0:
                    kb.wait("tensor", "st", lq)
                tm = kb.op("tensor", lambda e, tt=tt, hp=hp: e.matmul(
                    self.p_ab[:, hp * 512:(hp + 1) * 512],
                    self.c_p2[:, :], self.a_q[0][:, tt * 512:(tt + 1) * 512],
                    start=True, stop=True), inc=("t", 1))
                kb.wait("vector", "t", tm)
                kb.wait("vector", "st", lc)
                kb.wait_guard("vector", "a_ctx")
                kb.op("vector", lambda e, tt=tt: e.tensor_tensor(
                    self.a_f[0][:, :], self.a_q[0][:, tt * 512:(tt + 1) * 512],
                    self.a_e[:, tt * 512:(tt + 1) * 512], ALU.mult))
                kb.op("vector", lambda e, tt=tt, hp=hp: e.tensor_tensor(
                    self.a_f[1][:, :], self.p_ab[:, hp * 512:(hp + 1) * 512],
                    self.a_e[:, T + tt * 512:T + (tt + 1) * 512], ALU.mult))
                vm = kb.op("vector", lambda e, tt=tt: e.tensor_tensor(
                    self.a_ctx[:, tt * 512:(tt + 1) * 512],
                    self.a_f[0][:, :], self.a_f[1][:, :], ALU.add), inc=("v", 1))
                kb.set_guard(f"p_ab{hp}", "v", vm)
                vm_all = vm
            kb.wait("scalar", "v", vm_all)
            st = kb.dma("scalar", bufv[:, pb, :], self.a_ctx[:, 0:T], "st")
            kb.set_guard("a_ctx", "st", st)
            kb.set_guard("a_q0", "t", tm)
            kb.add_guard("a_q0", "v", vm_all)
            kb.set_guard(f"w:{self._nm(buf)}", "st", st)
        kb.set_guard("a_e", "v", self.kb.cnt["v"])

    def ffn_node(self, c, inputs_buf):
        kb = self.kb
        E = EDGES[c]
        def nsrc(i):
            return inputs_buf if i == -1 else self.nodes[i]
        wi = 0
        if c == 0:
            plist = [(E["eq"], E["in_q"], self.pq_buf)]
            pq, pk, pv = self.pq_buf, self.pq_buf, self.pq_buf
        elif c == 1:
            plist = [(E["eq"], E["in_q"], self.pq_buf), (E["ek"], E["in_k"], self.pk_buf)]
            pq, pk, pv = self.pq_buf, self.pk_buf, self.pq_buf
        else:
            plist = [(E["eq"], E["in_q"], self.pq_buf), (E["ek"], E["in_k"], self.pk_buf),
                     (E["ev"], E["in_v"], self.pv_buf)]
            pq, pk, pv = self.pq_buf, self.pk_buf, self.pv_buf
        for eidx, srcn, dstb in plist:
            self.proj(nsrc(srcn), self.We[eidx], dstb, wi=wi, be_idx=eidx)
            wi ^= 1
        self._ck(f"n{c}e")
        self.proj(pq, self.Wq[c], self.qh_buf, wi=wi, scale_ap=self.ew_ap(c, 0))
        wi ^= 1
        self.proj(pk, self.Wk[c], self.kh_buf, wi=wi, scale_ap=self.ew_ap(c, 1))
        wi ^= 1
        self.proj_tm(pv, self.Wv[c], wi=wi, scale_ap=self.ew_ap(c, 2))
        wi ^= 1
        self._ck(f"n{c}p")
        self.attn_all(H_IN, DH_IN, self.qh_buf, self.kh_buf, self.ctx_buf, RSQ_IN)
        self._ck(f"n{c}a")
        pqv = self.fm(self.pq_buf)
        aux_ld = {}
        def aux_load(q):
            kb.wait_guard("sync", "a_e")
            kb.wait("sync", "v", kb.cnt["v"])
            kb.wait_guard("sync", f"w:{self._nm(self.pq_buf)}")
            half = q % 2
            aux_ld[q] = kb.dma("sync",
                               self.a_e[:, half * 8192:(half + 1) * 8192]
                               .rearrange("p (k n) -> p k n", k=PB),
                               pqv[:, :, q * 1024:(q + 1) * 1024], "ld")
            kb.set_guard(f"r:{self._nm(self.pq_buf)}", "ld", aux_ld[q])
        ewq = self.ew_ap(c, 0)
        awc = self.aw_ap(c)
        def evac(kb_, mb, tl_abs, pslice, stage):
            q, tl = tl_abs // 2, tl_abs % 2
            base = (q % 2) * 8192
            kb.wait("vector", "ld", aux_ld[q])
            kb.op("vector", lambda e, mb=mb, tl=tl, base=base: e.tensor_scalar(
                self.a_f[2][:, :],
                self.a_e[:, base + mb * 1024 + tl * 512:base + mb * 1024 + tl * 512 + 512],
                ewq, None, ALU.mult))
            return kb.op("vector", lambda e, pslice=pslice, stage=stage: e.scalar_tensor_tensor(
                stage, pslice, awc, self.a_f[2][:, :], ALU.mult, ALU.add), inc=("v", 1))
        dst = self.nodes[c]
        self.proj(self.ctx_buf, self.Wo[c], dst, wi=wi, evac=evac, aux_load=aux_load)
        kb.set_guard("a_e", "v", kb.cnt["v"])

    def ffn(self, inputs_buf):
        for c in range(6):
            self.ffn_node(c, inputs_buf)
            self._ck(f"n{c}")
        self.boundary(None, self.nodes[5], None, None, self.sub_out, 1e-6,
                      affine=True, src_is_bf=True)

    def outer_attn(self):
        self.proj(self.ln_buf, self.wqkv[:, 0:D], self.qh_buf, wi=0)
        self.proj(self.ln_buf, self.wqkv[:, D:2 * D], self.kh_buf, wi=1)
        self.rope(self.qh_buf)
        self.proj_tm(self.ln_buf, self.wqkv[:, 2 * D:3 * D], wi=0, scale_ap=None)
        self.rope(self.kh_buf)
        self.attn_all(H_OUT, DH_OUT, self.qh_buf, self.kh_buf, self.ctx_buf, RSQ_OUT)
        self.proj(self.ctx_buf, self.w_o, self.sub_out, wi=1)

    def _ck(self, name):
        if self._stop == name:
            raise _StopBuild()

    def build(self, stop=None):
        kb = self.kb
        self._stop = stop
        try:
            self.emit_consts()
            self._ck("consts")
            self.boundary(self.x_in, None, None, None, self.ln_buf, 1e-5)
            self._ck("ln1")
            self.ffn(self.ln_buf)
            self._ck("ffnA")
            self.boundary(self.x_in, self.sub_out, 0.5, self.x_res, self.ln_buf, 1e-5)
            self._ck("bA")
            self.outer_attn()
            self._ck("outer")
            self.boundary(self.x_res, self.sub_out, 1.0, self.x_res, self.ln_buf, 1e-5)
            self.ffn(self.ln_buf)
            self._ck("ffnB")
            self.boundary(self.x_res, self.sub_out, 0.5, self.y_out, None, None)
        except _StopBuild:
            pass
        for sem in ("ld", "st", "wg", "t", "v", "s"):
            kb.wait("sync", sem, kb.cnt[sem])
        kb.finalize_checks()


class _StopBuild(Exception):
    pass


def build_program(stop=None, dbg=None):
    nc = bass.Bass("TRN2", target_bir_lowering=False)
    st = ExitStack()
    with st:
        prog = Prog(st, nc)
        if dbg is not None:
            dbg_out = nc.dram_tensor("dbg", [D, T], BF16, kind="ExternalOutput").ap()
        prog.build(stop=stop)
        if dbg is not None:
            kb = prog.kb
            if dbg.startswith("node"):
                src = prog.nodes[int(dbg[4:])]
            else:
                src = getattr(prog, dbg)
            if dbg == "vh_buf":
                src = src.rearrange("j p d -> (j p) d")
            kb.wait("gpsimd", "st", kb.cnt["st"])
            kb.wait("gpsimd", "ld", kb.cnt["ld"])
            fin = kb.dma("gpsimd", dbg_out, src, "wg")
            kb.wait("gpsimd", "wg", fin)
        with nc.Block() as block:
            prog.kb.emit_all(block)
    return nc


def _rope_tables():
    inv_freq = 1.0 / (10000.0 ** (np.arange(0, DH_OUT, 2, dtype=np.float64) / DH_OUT))
    t = np.arange(T, dtype=np.float64)
    freqs = np.outer(t, inv_freq)
    emb = np.concatenate([freqs, freqs], axis=-1)
    cos = np.cos(emb).astype(np.float32)
    sin = np.sin(emb).astype(np.float32)
    cosfm = np.ascontiguousarray(np.tile(cos.T, (2, 1))).astype(ml_dtypes.bfloat16)
    sinfm = np.ascontiguousarray(np.tile(sin.T, (2, 1))).astype(ml_dtypes.bfloat16)
    return cosfm, sinfm


def _p2_const():
    P = np.zeros((64, 64), dtype=np.float32)
    for dd in range(32):
        P[dd, 32 + dd] = -1.0
        P[32 + dd, dd] = 1.0
    blk = np.zeros((128, 128), dtype=np.float32)
    blk[0:64, 0:64] = P
    blk[64:128, 64:128] = P
    return np.ascontiguousarray(blk.T).astype(ml_dtypes.bfloat16)


def make_in_maps(L0_attn_x, L0_attn_w_qkv, L0_attn_w_o, L0_ffn_ew, L0_ffn_act_w,
                 L0_ffn_We, L0_ffn_be, L0_ffn_Wq, L0_ffn_Wk, L0_ffn_Wv, L0_ffn_Wo,
                 L0_ffn_ln_g, L0_ffn_ln_b):
    x = np.asarray(L0_attn_x, dtype=np.float32)
    wqkv = np.ascontiguousarray(np.asarray(L0_attn_w_qkv, dtype=np.float32))
    w_o = np.ascontiguousarray(np.asarray(L0_attn_w_o, dtype=np.float32))
    We = np.ascontiguousarray(np.asarray(L0_ffn_We, dtype=np.float32))
    Wq = np.ascontiguousarray(np.asarray(L0_ffn_Wq, dtype=np.float32))
    Wk = np.ascontiguousarray(np.asarray(L0_ffn_Wk, dtype=np.float32))
    Wv = np.ascontiguousarray(np.asarray(L0_ffn_Wv, dtype=np.float32))
    Wo = np.ascontiguousarray(np.asarray(L0_ffn_Wo, dtype=np.float32))
    be = np.asarray(L0_ffn_be, dtype=np.float32)
    ew = np.asarray(L0_ffn_ew, dtype=np.float32)
    act_w = np.asarray(L0_ffn_act_w, dtype=np.float32)
    ln_g = np.asarray(L0_ffn_ln_g, dtype=np.float32)
    ln_b = np.asarray(L0_ffn_ln_b, dtype=np.float32)

    befm = np.ascontiguousarray(be.reshape(20, 8, 128).transpose(2, 0, 1).reshape(128, 160))
    gfm = ln_g.reshape(8, 128).T
    bfm = ln_b.reshape(8, 128).T
    gbfm = np.ascontiguousarray(np.concatenate([gfm, bfm], axis=1))
    sca = np.concatenate([ew.reshape(-1), act_w.reshape(-1)]).reshape(1, 24)
    sca = np.ascontiguousarray(sca)
    cosfm, sinfm = _rope_tables()
    p2 = _p2_const()

    in_maps = []
    for b in range(x.shape[0]):
        in_maps.append(dict(
            x=np.ascontiguousarray(x[b].T),
            wqkv=wqkv, w_o=w_o, We=We, Wqm=Wq, Wkm=Wk, Wvm=Wv, Wom=Wo,
            befm=befm, gbfm=gbfm, sca=sca, cosbf=cosfm, sinbf=sinfm, p2=p2,
        ))
    return in_maps


def kernel(**inputs):
    in_maps = make_in_maps(**inputs)
    nc = build_program()
    res = run_bass_kernel_spmd(nc, in_maps, core_ids=list(range(8)))
    out = np.stack([np.asarray(r["y"]).T for r in res.results], axis=0)
    return np.ascontiguousarray(out)


# version 19
# speedup vs baseline: 1.2119x; 1.2119x over previous
import math
from contextlib import ExitStack

import numpy as np
import ml_dtypes

import concourse.bass as bass
import concourse.mybir as mybir
from concourse.bass_utils import run_bass_kernel_spmd

F32 = mybir.dt.float32
BF16 = mybir.dt.bfloat16
AF = mybir.ActivationFunctionType
ALU = mybir.AluOpType
AX = mybir.AxisListType

D = 1024
T = 4096
PB = 8
NT = 8
TS = 512
NSB = 32
H_IN, DH_IN = 8, 128
H_OUT, DH_OUT = 16, 64
RSQ_IN = 1.0 / math.sqrt(DH_IN)
RSQ_OUT = 1.0 / math.sqrt(DH_OUT)
ENGS = ("tensor", "vector", "scalar", "sync", "gpsimd")

def ffn_edges():
    out = []
    lind = 0
    for c in range(6):
        nsrc = min(c + 1, 5)
        snode = c - nsrc
        eq, ek, ev = lind + nsrc - 1, lind, lind + nsrc // 2
        in_q, in_k, in_v = c - 1, snode, snode + nsrc // 2
        out.append(dict(eq=eq, ek=ek, ev=ev, in_q=in_q, in_k=in_k, in_v=in_v))
        lind += nsrc
    return out

EDGES = ffn_edges()

class KB:
    def __init__(self, nc):
        self.nc = nc
        self.ops = {e: [] for e in ENGS}
        self.sems = {}
        self.cnt = {}
        self.last_wait = {e: {} for e in ENGS}
        self.max_waited = {}
        self.guard = {}
        self.nops = 0

    def add_sem(self, handle, name):
        self.sems[name] = handle
        self.cnt[name] = 0
        self.max_waited[name] = 0

    def bump(self, sem, k):
        self.cnt[sem] += k
        return self.cnt[sem]

    def wait(self, eng, sem, val):
        if val is None or val <= 0:
            return
        assert sem in self.sems, sem
        self.max_waited[sem] = max(self.max_waited[sem], val)
        if self.last_wait[eng].get(sem, 0) >= val:
            return
        self.last_wait[eng][sem] = val
        s = self.sems[sem]
        self.ops[eng].append(lambda e, s=s, v=val: e.wait_ge(s, v))
        self.nops += 1

    def wait_guard(self, eng, res):
        for sem, val in self.guard.get(res, ()):
            self.wait(eng, sem, val)

    def set_guard(self, res, sem, val):
        self.guard[res] = [(sem, val)]

    def add_guard(self, res, sem, val):
        self.guard.setdefault(res, []).append((sem, val))

    def op(self, eng, fn, inc=None):
        self.nops += 1
        if inc is None:
            self.ops[eng].append(fn)
            return None
        sem, k = inc
        v = self.bump(sem, k)
        s = self.sems[sem]
        def wrapped(e, fn=fn, s=s, k=k):
            fn(e).then_inc(s, k)
        self.ops[eng].append(wrapped)
        return v

    def dma(self, q, out, in_, sem, **kw):
        return self.op(
            q, lambda e, out=out, in_=in_, kw=kw: e.dma_start(out=out, in_=in_, **kw),
            inc=(sem, 16))

    def finalize_checks(self):
        for name, mx in self.max_waited.items():
            assert mx <= self.cnt[name], (
                f"sem {name}: waited {mx} > final count {self.cnt[name]} (hang)")

    def emit_all(self, block):
        for eng in ENGS:
            ops = self.ops[eng]
            def runner(e, ops=ops):
                for f in ops:
                    f(e)
            getattr(block, eng)(runner)


class Prog:

    def __init__(self, st, nc):
        self.st = st
        self.nc = nc
        mk = lambda f, *a, **k: st.enter_context(f(*a, **k))
        dt = nc.dram_tensor

        self.x_in = dt("x", [D, T], F32, kind="ExternalInput").ap()
        self.wqkv = dt("wqkv", [D, 3 * D], F32, kind="ExternalInput").ap()
        self.w_o = dt("w_o", [D, D], F32, kind="ExternalInput").ap()
        self.We = dt("We", [20, D, D], F32, kind="ExternalInput").ap()
        self.Wq = dt("Wqm", [6, D, D], F32, kind="ExternalInput").ap()
        self.Wk = dt("Wkm", [6, D, D], F32, kind="ExternalInput").ap()
        self.Wv = dt("Wvm", [6, D, D], F32, kind="ExternalInput").ap()
        self.Wo = dt("Wom", [6, D, D], F32, kind="ExternalInput").ap()
        self.befm = dt("befm", [128, 160], F32, kind="ExternalInput").ap()
        self.gbfm = dt("gbfm", [128, 16], F32, kind="ExternalInput").ap()
        self.sca = dt("sca", [1, 24], F32, kind="ExternalInput").ap()
        self.cosbf = dt("cosbf", [128, T], BF16, kind="ExternalInput").ap()
        self.sinbf = dt("sinbf", [128, T], BF16, kind="ExternalInput").ap()
        self.p2 = dt("p2", [128, 128], BF16, kind="ExternalInput").ap()
        self.y_out = dt("y", [D, T], F32, kind="ExternalOutput").ap()

        self.x_res = dt("x_res", [D, T], F32).ap()
        self.ln_buf = dt("ln_buf", [D, T], BF16).ap()
        self.nodes = [dt(f"node{i}", [D, T], BF16).ap() for i in range(6)]
        self.pq_buf = dt("pq_buf", [D, T], BF16).ap()
        self.pk_buf = dt("pk_buf", [D, T], BF16).ap()
        self.pv_buf = dt("pv_buf", [D, T], BF16).ap()
        self.qh_buf = dt("qh_buf", [D, T], BF16).ap()
        self.kh_buf = dt("kh_buf", [D, T], BF16).ap()
        self.vh_buf = dt("vh_buf", [NSB, 128, D], BF16).ap()
        self.ctx_buf = dt("ctx_buf", [D, T], BF16).ap()
        self.sub_out = dt("sub_out", [D, T], BF16).ap()

        self.a_src = mk(nc.sbuf_tensor, "a_src", [128, 16384], BF16)
        self.a_e = mk(nc.sbuf_tensor, "a_e", [128, 16384], BF16)
        self.a_q = [mk(nc.sbuf_tensor, f"a_q{i}", [128, T], BF16) for i in range(2)]
        self.a_k = [mk(nc.sbuf_tensor, f"a_k{i}", [128, T], BF16) for i in range(2)]
        self.a_v = [mk(nc.sbuf_tensor, f"a_v{i}", [128, T], BF16) for i in range(2)]
        self.a_ctx = mk(nc.sbuf_tensor, "a_ctx", [128, T], BF16)
        self.a_x = mk(nc.sbuf_tensor, "a_x", [128, T], F32)
        self.a_y = mk(nc.sbuf_tensor, "a_y", [128, T], F32)
        self.wb = [mk(nc.sbuf_tensor, f"wb{i}", [128, 8192], BF16) for i in range(2)]
        self.a_st = [mk(nc.sbuf_tensor, f"ast{i}", [128, 2048], BF16) for i in range(2)]
        self.a_f = [mk(nc.sbuf_tensor, f"af{i}", [128, 512], F32) for i in range(5)]
        self.c_be = mk(nc.sbuf_tensor, "c_be", [128, 160], F32)
        self.c_gb = mk(nc.sbuf_tensor, "c_gb", [128, 16], F32)
        self.c_sca_row = mk(nc.sbuf_tensor, "c_sca_row", [1, 24], F32)
        self.c_scb = mk(nc.sbuf_tensor, "c_scb", [128, 24], F32)
        self.c_p2 = mk(nc.sbuf_tensor, "c_p2", [128, 128], BF16)
        self.ones_bf = mk(nc.sbuf_tensor, "ones_bf", [128, 128], BF16)
        self.oneD_bf = mk(nc.sbuf_tensor, "oneD_bf", [128, 128], BF16)
        self.ones1f = mk(nc.sbuf_tensor, "ones1f", [1, 128], F32)
        self.ones_f = mk(nc.sbuf_tensor, "ones_f", [128, 128], F32)
        self.c_stat = mk(nc.sbuf_tensor, "c_stat", [128, 32], F32)
        self.c_bias = mk(nc.sbuf_tensor, "c_bias", [128, 8], F32)
        self.c_eps = mk(nc.sbuf_tensor, "c_eps", [128, 2], F32)

        self.p_rot = mk(nc.psum_tensor, "p_rot", [128, 2048], F32)
        self.p_ctx = mk(nc.psum_tensor, "p_ctx", [128, 512], F32)
        self.p_z = mk(nc.psum_tensor, "p_z", [128, 512], F32)
        self.p_ab = mk(nc.psum_tensor, "p_ab", [128, 1024], F32)

        self.kb = KB(nc)
        for name in ("ld", "st", "wg", "t", "v", "s"):
            self.kb.add_sem(mk(nc.semaphore, f"sem_{name}"), name)

        self.wb_guard = [None, None]
        self.SC_EW = lambda c, j: c * 3 + j
        self.SC_AW = lambda c: 18 + c

    def fm(self, dram_ap):
        return dram_ap.rearrange("(b p) t -> p b t", p=128)

    def ew_ap(self, c, j):
        col = self.SC_EW(c, j)
        return self.c_scb[:, col:col + 1]

    def aw_ap(self, c):
        col = self.SC_AW(c)
        return self.c_scb[:, col:col + 1]

    def emit_consts(self):
        kb = self.kb
        kb.dma("sync", self.c_be[:, :], self.befm, "ld")
        kb.dma("sync", self.c_gb[:, :], self.gbfm, "ld")
        kb.dma("sync", self.c_sca_row[:, :], self.sca, "ld")
        l6 = kb.dma("sync", self.c_p2[:, :], self.p2, "ld")
        kb.op("vector", lambda e: e.memset(self.ones_bf[:, :], 1.0))
        kb.op("vector", lambda e: e.memset(self.oneD_bf[:, :], 1.0 / 1024.0))
        kb.op("vector", lambda e: e.memset(self.c_eps[:, 0:1], 1e-5))
        kb.op("vector", lambda e: e.memset(self.c_eps[:, 1:2], 1e-6))
        kb.op("vector", lambda e: e.memset(self.ones_f[:, :], 1.0))
        v1 = kb.op("vector", lambda e: e.memset(self.ones1f[:, :], 1.0), inc=("v", 1))
        kb.wait("tensor", "ld", l6)
        kb.wait("tensor", "v", v1)
        t = None
        for j in range(24):
            t = kb.op("tensor",
                      lambda e, j=j: e.matmul(self.p_ab[:, j:j + 1], self.ones1f[:1, :],
                                              self.c_sca_row[:1, j:j + 1],
                                              start=True, stop=True),
                      inc=("t", 1) if j == 23 else None)
        kb.wait("vector", "t", t)
        v2 = kb.op("vector", lambda e: e.tensor_copy(self.c_scb[:, :], self.p_ab[:, 0:24]),
                   inc=("v", 1))
        kb.set_guard("p_ab0", "v", v2)
        kb.set_guard("p_ab1", "v", v2)

        for eng in ("vector", "scalar", "tensor"):
            kb.wait(eng, "ld", l6)
        kb.wait("scalar", "v", v1)
        kb.wait("tensor", "v", v1)

    def load_weight(self, w_ap, wi, dout=D):
        kb = self.kb
        if self.wb_guard[wi] is not None:
            sem, val = self.wb_guard[wi]
            kb.wait("gpsimd", sem, val)
        r = w_ap.rearrange("(k p) n -> p k n", p=128)
        out = self.wb[wi][:, 0:8 * dout].rearrange("p (k n) -> p k n", k=PB)
        return kb.dma("gpsimd", out, r, "wg")

    def proj(self, src, w_ap, dst, wi, scale_ap=None, be_idx=None, evac=None,
             aux_load=None, dout_blocks=PB):
        kb = self.kb
        wld = self.load_weight(w_ap, wi, dout=dout_blocks * 128)
        srcv = self.fm(src)
        dstv = self.fm(dst)
        sname, dname = self._nm(src), self._nm(dst)
        ldq = {}

        def issue_load(q):
            half = q % 2
            kb.wait_guard("sync", f"a_srcH{half}")
            if f"w:{sname}:q{q}" in kb.guard:
                kb.wait_guard("sync", f"w:{sname}:q{q}")
            else:
                kb.wait_guard("sync", f"w:{sname}")
            ldq[q] = kb.dma("sync",
                            self.a_src[:, half * 8192:(half + 1) * 8192]
                            .rearrange("p (k n) -> p k n", k=PB),
                            srcv[:, :, q * 1024:(q + 1) * 1024], "ld")
            kb.set_guard(f"r:{sname}", "ld", ldq[q])

        issue_load(0)
        if aux_load is not None:
            aux_load(0)
        bank = 0
        for q in range(4):
            half = q % 2
            base = half * 8192
            if q + 1 < 4:
                issue_load(q + 1)
                if aux_load is not None:
                    aux_load(q + 1)
            first_mm = True
            for mb in range(dout_blocks):
                si = mb % 2
                for tl in range(2):
                    bank = (bank + 1) % 4
                    kb.wait_guard("tensor", f"p_rot{bank}")
                    if first_mm:
                        kb.wait("tensor", "ld", ldq[q])
                        if q == 0:
                            kb.wait("tensor", "wg", wld)
                        first_mm = False
                    pslice = self.p_rot[:, bank * 512:(bank + 1) * 512]
                    tmil = None
                    dw = dout_blocks * 128
                    for k in range(PB):
                        tmil = kb.op(
                            "tensor",
                            lambda e, k=k, mb=mb, tl=tl, pslice=pslice, dw=dw, base=base:
                            e.matmul(
                                pslice,
                                self.wb[wi][:, k * dw + mb * 128:k * dw + mb * 128 + 128],
                                self.a_src[:, base + k * 1024 + tl * 512:
                                            base + k * 1024 + tl * 512 + 512],
                                start=(k == 0), stop=(k == PB - 1)),
                            inc=("t", 1) if k == PB - 1 else None)
                    kb.wait("vector", "t", tmil)
                    kb.wait_guard("vector", f"a_st{si}")
                    stage = self.a_st[si][:, tl * 512:(tl + 1) * 512]
                    if evac is not None:
                        vmil = evac(kb, mb, q * 2 + tl, pslice, stage)
                    elif be_idx is not None:
                        be_ap = self.c_be[:, be_idx * 8 + mb:be_idx * 8 + mb + 1]
                        vmil = kb.op("vector",
                                     lambda e, pslice=pslice, stage=stage, be_ap=be_ap:
                                     e.tensor_scalar(stage, pslice, be_ap, None, ALU.add),
                                     inc=("v", 1))
                    elif scale_ap is not None:
                        vmil = kb.op("vector",
                                     lambda e, pslice=pslice, stage=stage, s1=scale_ap:
                                     e.tensor_scalar(stage, pslice, s1, None, ALU.mult),
                                     inc=("v", 1))
                    else:
                        vmil = kb.op("vector",
                                     lambda e, pslice=pslice, stage=stage:
                                     e.tensor_copy(stage, pslice),
                                     inc=("v", 1))
                    kb.set_guard(f"p_rot{bank}", "v", vmil)
                kb.wait("scalar", "v", vmil)
                kb.wait_guard("scalar", f"r:{dname}")
                smil = kb.dma("scalar", dstv[:, mb, q * 1024:(q + 1) * 1024],
                              self.a_st[si][:, 0:1024], "st")
                kb.set_guard(f"a_st{si}", "st", smil)
            kb.set_guard(f"a_srcH{half}", "t", tmil)
            kb.set_guard(f"w:{dname}:q{q}", "st", smil)
        kb.set_guard(f"w:{dname}", "st", smil)
        self.wb_guard[wi] = ("t", self.kb.cnt["t"])
        return smil

    def proj_tm(self, src, w_ap, wi, scale_ap, interleave=None):
        kb = self.kb
        wld = self.load_weight(w_ap, wi)
        srcv = self.fm(src)
        sname = self._nm(src)
        vhv = self.vh_buf.rearrange("j p d -> p j d")
        ldq = {}

        def issue_load(q):
            half = q % 2
            kb.wait_guard("sync", f"a_srcH{half}")
            if f"w:{sname}:q{q}" in kb.guard:
                kb.wait_guard("sync", f"w:{sname}:q{q}")
            else:
                kb.wait_guard("sync", f"w:{sname}")
            ldq[q] = kb.dma("sync",
                            self.a_src[:, half * 8192:(half + 1) * 8192]
                            .rearrange("p (k n) -> p k n", k=PB),
                            srcv[:, :, q * 1024:(q + 1) * 1024], "ld")
            kb.set_guard(f"r:{sname}", "ld", ldq[q])

        issue_load(0)
        bank = 0
        for q in range(4):
            half = q % 2
            base = half * 8192
            if q + 1 < 4:
                issue_load(q + 1)
            first_mm = True
            for sbl in range(8):
                sb = q * 8 + sbl
                si = sbl % 2
                for dt_ in (0, 1):
                    bank = (bank + 1) % 4
                    kb.wait_guard("tensor", f"p_rot{bank}")
                    if first_mm:
                        kb.wait("tensor", "ld", ldq[q])
                        if q == 0:
                            kb.wait("tensor", "wg", wld)
                        first_mm = False
                    pslice = self.p_rot[:, bank * 512:(bank + 1) * 512]
                    tmil = None
                    for k in range(PB):
                        tmil = kb.op(
                            "tensor",
                            lambda e, k=k, sbl=sbl, dt_=dt_, pslice=pslice, base=base:
                            e.matmul(
                                pslice,
                                self.a_src[:, base + k * 1024 + sbl * 128:
                                            base + k * 1024 + sbl * 128 + 128],
                                self.wb[wi][:, k * 1024 + dt_ * 512:k * 1024 + dt_ * 512 + 512],
                                start=(k == 0), stop=(k == PB - 1)),
                            inc=("t", 1) if k == PB - 1 else None)
                    kb.wait("vector", "t", tmil)
                    kb.wait_guard("vector", f"a_st{si}")
                    stage = self.a_st[si][:, dt_ * 512:(dt_ + 1) * 512]
                    if scale_ap is not None:
                        vmil = kb.op("vector",
                                     lambda e, pslice=pslice, stage=stage, s1=scale_ap:
                                     e.tensor_scalar(stage, pslice, s1, None, ALU.mult),
                                     inc=("v", 1))
                    else:
                        vmil = kb.op("vector",
                                     lambda e, pslice=pslice, stage=stage:
                                     e.tensor_copy(stage, pslice), inc=("v", 1))
                    kb.set_guard(f"p_rot{bank}", "v", vmil)
                kb.wait("scalar", "v", vmil)
                kb.wait_guard("scalar", "r:vh_buf")
                smil = kb.dma("scalar", vhv[:, sb:sb + 1, :],
                              self.a_st[si][:, 0:1024].rearrange("p (j d) -> p j d", j=1),
                              "st")
                kb.set_guard(f"a_st{si}", "st", smil)
                if interleave is not None and sb % 2 == 1:
                    next(interleave, None)
            kb.set_guard(f"a_srcH{half}", "t", tmil)
        if interleave is not None:
            for _ in interleave:
                pass
        kb.set_guard("w:vh_buf", "st", smil)
        self.wb_guard[wi] = ("t", self.kb.cnt["t"])
        return smil

    def load_head(self, h, dh, slot, qh_src, kh_src):
        kb = self.kb
        kb.wait_guard("sync", f"a_q{slot}")
        kb.wait_guard("sync", f"w:{self._nm(qh_src)}")
        lq = kb.dma("sync", self.a_q[slot][0:dh, :], qh_src[h * dh:(h + 1) * dh, :], "ld")
        kb.set_guard(f"r:{self._nm(qh_src)}", "ld", lq)
        kb.wait_guard("sync", f"a_k{slot}")
        kb.wait_guard("sync", f"w:{self._nm(kh_src)}")
        lk = kb.dma("sync", self.a_k[slot][0:dh, :], kh_src[h * dh:(h + 1) * dh, :], "ld")
        kb.set_guard(f"r:{self._nm(kh_src)}", "ld", lk)
        kb.wait_guard("sync", f"a_v{slot}")
        kb.wait_guard("sync", "w:vh_buf")
        vhv = self.vh_buf[:, :, h * dh:(h + 1) * dh].rearrange("j p d -> p j d")
        av = self.a_v[slot][:, 0:NSB * dh].rearrange("p (j d) -> p j d", d=dh)
        lv = None
        for part in range(4):
            lv = kb.dma("sync", av[:, part * 8:(part + 1) * 8, :],
                        vhv[:, part * 8:(part + 1) * 8, :], "ld")
        kb.set_guard("r:vh_buf", "ld", lv)
        return lq, lk, lv

    def stats_head(self, h, dh, slot, rsq, lq, lk):
        kb = self.kb
        cs = slot * 16
        tprev = None
        for side, (buf, lbuf) in enumerate(((self.a_q[slot], lq), (self.a_k[slot], lk))):
            for chunk in range(2):
                si = chunk % 2
                kb.wait("vector", "ld", lbuf)
                kb.wait_guard("vector", f"a_st{si}")
                vsq = kb.op("vector", lambda e, buf=buf, chunk=chunk, si=si:
                            e.tensor_tensor(self.a_st[si][0:dh, 0:2048],
                                            buf[0:dh, chunk * 2048:(chunk + 1) * 2048],
                                            buf[0:dh, chunk * 2048:(chunk + 1) * 2048],
                                            ALU.mult), inc=("v", 1))
                for ti in range(4):
                    tt = chunk * 4 + ti
                    hp = tt % 2
                    kb.wait_guard("tensor", f"p_ab{hp}")
                    if ti == 0:
                        kb.wait("tensor", "v", vsq)
                    tprev = kb.op("tensor", lambda e, ti=ti, hp=hp, si=si: e.matmul(
                        self.p_ab[:, hp * 512:(hp + 1) * 512],
                        self.ones_bf[0:dh, :],
                        self.a_st[si][0:dh, ti * 512:(ti + 1) * 512],
                        start=True, stop=True), inc=("t", 1))
                    kb.wait("vector", "t", tprev)
                    vm = kb.op("vector", lambda e, tt=tt, hp=hp, side=side, cs=cs:
                               e.reduce_max(self.c_stat[:, cs + side * 8 + tt:
                                                        cs + side * 8 + tt + 1],
                                            self.p_ab[:, hp * 512:(hp + 1) * 512],
                                            axis=AX.X), inc=("v", 1))
                    kb.set_guard(f"p_ab{hp}", "v", vm)
                kb.set_guard(f"a_st{si}", "t", tprev)
        b0, b1, b2, b3 = slot * 4, slot * 4 + 1, slot * 4 + 2, slot * 4 + 3
        kb.wait("vector", "v", kb.cnt["v"])
        kb.op("vector", lambda e, cs=cs, b0=b0: e.reduce_max(
            self.c_bias[:, b0:b0 + 1], self.c_stat[:, cs:cs + 8], axis=AX.X))
        r2 = kb.op("vector", lambda e, cs=cs, b1=b1: e.reduce_max(
            self.c_bias[:, b1:b1 + 1], self.c_stat[:, cs + 8:cs + 16], axis=AX.X),
            inc=("v", 1))
        kb.wait("vector", "v", r2)
        vmil = kb.op("vector", lambda e, b0=b0, b1=b1, b3=b3: e.tensor_tensor(
            self.c_bias[:, b3:b3 + 1], self.c_bias[:, b0:b0 + 1],
            self.c_bias[:, b1:b1 + 1], ALU.mult), inc=("v", 1))
        kb.wait("scalar", "v", vmil)
        kb.op("scalar", lambda e, b1=b1, b3=b3: e.activation(
            self.c_bias[:, b1:b1 + 1], self.c_bias[:, b3:b3 + 1], AF.Ln))
        smil = kb.op("scalar", lambda e, b0=b0, b1=b1: e.activation(
            self.c_bias[:, b0:b0 + 1], self.c_bias[:, b1:b1 + 1], AF.Exp,
            scale=0.5), inc=("s", 1))
        kb.wait("vector", "s", smil)
        vmil = kb.op("vector", lambda e, rsq=rsq, b0=b0, b2=b2: e.tensor_scalar(
            self.c_bias[:, b2:b2 + 1], self.c_bias[:, b0:b0 + 1], -rsq, None,
            ALU.mult), inc=("v", 1))
        return vmil

    def attn_main(self, h, dh, slot, ctx_dst, rsq, bias_ready, lv):
        kb = self.kb
        NZT = 12
        aq, ak, av_ = self.a_q[slot], self.a_k[slot], self.a_v[slot]
        bias_ap = self.c_bias[:, slot * 4 + 2:slot * 4 + 3]
        ctx_done = None
        vm = None
        if not hasattr(self, "_pair_ctr"):
            self._pair_ctr = 0
        for tt in range(NT):
            exp_done = {}
            e_guard_new = {}
            first_sc = True
            vadd = None

            pairs = [(("p_rot0", "p_rot1"), self.p_rot[:, 0:1024]),
                     (("p_rot2", "p_rot3"), self.p_rot[:, 1024:2048]),
                     (("p_ab0", "p_ab1"), self.p_ab[:, 0:1024])]

            def emit_scores(g, tt=tt):
                nonlocal first_sc
                self._pair_ctr = (self._pair_ctr + 1) % 3
                bnames, pap = pairs[self._pair_ctr]
                for b in bnames:
                    kb.wait_guard("tensor", b)
                if first_sc:
                    kb.wait("tensor", "ld", lv)
                    first_sc = False
                tm = None
                for jj in range(2):
                    j = 2 * g + jj
                    tm = kb.op("tensor", lambda e, j=j, jj=jj, pap=pap, tt=tt: e.matmul(
                        pap[:, jj * 512:(jj + 1) * 512],
                        ak[0:dh, j * 128:(j + 1) * 128],
                        aq[0:dh, tt * 512:(tt + 1) * 512],
                        start=True, stop=True), inc=("t", 1) if jj == 1 else None)
                kb.wait("scalar", "t", tm)
                prevm = self.e_guard.get(g)
                if prevm is not None:
                    for sem, val in prevm:
                        kb.wait("scalar", sem, val)
                else:
                    kb.wait_guard("scalar", "a_e")
                if g == 0:
                    kb.wait("scalar", "v", bias_ready)
                sm = kb.op("scalar", lambda e, g=g, pap=pap: e.activation(
                    self.a_e[:, g * 1024:(g + 1) * 1024],
                    pap[:, :], AF.Exp,
                    bias=bias_ap, scale=rsq), inc=("s", 1))
                for b in bnames:
                    kb.set_guard(b, "s", sm)
                exp_done[g] = sm

            def emit_ctxz(g, tt=tt):
                nonlocal vadd
                kb.wait("tensor", "s", exp_done[g])
                if g == 0:
                    kb.wait_guard("tensor", "p_ctx")
                tm = None
                for jj in range(2):
                    j = 2 * g + jj
                    tm = kb.op("tensor", lambda e, j=j: e.matmul(
                        self.p_ctx[0:dh, :],
                        av_[:, j * dh:(j + 1) * dh],
                        self.a_e[:, j * 512:(j + 1) * 512],
                        start=(j == 0), stop=(j == NSB - 1),
                        skip_group_check=True),
                        inc=("t", 1) if (jj == 1 and g < NZT) else None)
                    if g >= NZT:
                        if g == NZT and jj == 0:
                            kb.wait_guard("tensor", "p_z")
                        tm = kb.op("tensor", lambda e, j=j, g=g, jj=jj: e.matmul(
                            self.p_z[:, :],
                            self.ones_bf[:, :],
                            self.a_e[:, j * 512:(j + 1) * 512],
                            start=(g == NZT and jj == 0), stop=False,
                            skip_group_check=True), inc=("t", 1) if jj == 1 else None)
                e_guard_new[g] = [("t", tm)]
                if g < NZT:
                    kb.wait("vector", "s", exp_done[g])
                    kb.wait("vector", "t", tm)
                    if g == 0:
                        kb.wait_guard("vector", "a_fz")
                        vadd = kb.op("vector", lambda e: e.tensor_tensor(
                            self.a_f[3][:, :], self.a_e[:, 0:512],
                            self.a_e[:, 512:1024], ALU.add), inc=("v", 1))
                    else:
                        for jj in range(2):
                            j = 2 * g + jj
                            vadd = kb.op("vector", lambda e, j=j: e.tensor_tensor(
                                self.a_f[3][:, :], self.a_f[3][:, :],
                                self.a_e[:, j * 512:(j + 1) * 512], ALU.add),
                                inc=("v", 1) if jj == 1 else None)
                    e_guard_new[g] = [("v", vadd)]
                return tm

            emit_scores(0)
            if self.pending_norm is not None:
                self.pending_norm()
                self.pending_norm = None
            emit_scores(1)
            emit_scores(2)
            for g in range(3, 16):
                emit_ctxz(g - 3)
                emit_scores(g)
            emit_ctxz(13)
            emit_ctxz(14)
            emit_ctxz(15)
            kb.wait("tensor", "v", vadd)
            ctx_done = kb.op("tensor", lambda e: e.matmul(
                self.p_z[:, :], self.ones_f[:, :], self.a_f[3][:, :],
                start=False, stop=True, skip_group_check=True), inc=("t", 1))
            kb.set_guard("a_fz", "t", ctx_done)
            self.e_guard = dict(e_guard_new)

            def norm(tt=tt, ctx_done=ctx_done, dh=dh, h=h, slot=slot,
                     ctx_dst=ctx_dst):
                kb.wait("scalar", "t", ctx_done)
                kb.wait_guard("scalar", "a_f2")
                sm = kb.op("scalar", lambda e: e.activation(
                    self.a_f[0][:, :], self.p_z[:, :], AF.Ln), inc=("s", 1))
                kb.set_guard("p_z", "s", sm)
                sm = kb.op("scalar", lambda e: e.activation(
                    self.a_f[1][:, :], self.a_f[0][:, :], AF.Exp, scale=-1.0),
                    inc=("s", 1))
                kb.wait("vector", "t", ctx_done)
                kb.wait_guard("vector", "a_f3")
                cvm = kb.op("vector", lambda e: e.tensor_copy(
                    self.a_f[2][0:dh, :], self.p_ctx[0:dh, :]), inc=("v", 1))
                kb.set_guard("p_ctx", "v", cvm)
                kb.wait("vector", "s", sm)
                kb.wait_guard("vector", "a_ctx")
                vm = kb.op("vector", lambda e, tt=tt: e.tensor_tensor(
                    self.a_ctx[0:dh, tt * 512:(tt + 1) * 512],
                    self.a_f[2][0:dh, :], self.a_f[1][0:dh, :], ALU.mult),
                    inc=("v", 1))
                kb.set_guard("a_f2", "v", vm)
                kb.set_guard("a_f3", "v", vm)
                if tt == NT - 1:
                    kb.wait("gpsimd", "v", vm)
                    kb.wait_guard("gpsimd", f"r:{self._nm(ctx_dst)}")
                    st = kb.dma("gpsimd", ctx_dst[h * dh:(h + 1) * dh, :],
                                self.a_ctx[0:dh, :], "st")
                    kb.set_guard("a_ctx", "st", st)
                    kb.set_guard(f"w:{self._nm(ctx_dst)}", "st", st)
            self.pending_norm = norm

        kb.set_guard(f"a_q{slot}", "t", ctx_done)
        kb.set_guard(f"a_k{slot}", "t", ctx_done)
        kb.set_guard(f"a_v{slot}", "t", ctx_done)
        return ctx_done

    def attn_all(self, nheads, dh, qh_src, kh_src, ctx_dst, rsq):
        self.e_guard = {}
        self.pending_norm = None
        pend = {}
        pend[0] = self.load_head(0, dh, 0, qh_src, kh_src)
        bias = {0: self.stats_head(0, dh, 0, rsq, pend[0][0], pend[0][1])}
        last = None
        for h in range(nheads):
            if h + 1 < nheads:
                slot = (h + 1) % 2
                pend[h + 1] = self.load_head(h + 1, dh, slot, qh_src, kh_src)
                bias[h + 1] = self.stats_head(h + 1, dh, slot, rsq,
                                              pend[h + 1][0], pend[h + 1][1])
            last = self.attn_main(h, dh, h % 2, ctx_dst, rsq, bias[h], pend[h][2])
        self.pending_norm()
        self.pending_norm = None
        self.kb.set_guard("a_e", "t", last)

    def boundary(self, x_src, sub_src, coef, x_dst, ln_dst, eps, affine=False,
                 src_is_bf=False):
        kb = self.kb
        xsv = self.fm(x_src) if x_src is not None else None
        subv = self.fm(sub_src) if sub_src is not None else None
        xdv = self.fm(x_dst) if x_dst is not None else None
        lnv = self.fm(ln_dst) if ln_dst is not None else None
        actxf = self.a_ctx[:, :].bitcast(F32)
        astf = self.a_st[0][:, :].bitcast(F32)
        stats = [
            [self.a_f[i][:, :] for i in range(5)],
            [actxf[:, i * 512:(i + 1) * 512] for i in range(4)]
            + [astf[:, 0:512]],
        ]
        pp_name = [("p_ab0", "p_ab1"), ("p_rot0", "p_rot1")]
        pp_ap = [self.p_ab, self.p_rot[:, 0:1024]]
        if eps is not None:
            eps_ap = self.c_eps[:, 0:1] if eps == 1e-5 else self.c_eps[:, 1:2]
        for tt in range(NT):
            p = tt % 2
            xs = self.a_x if p == 0 else self.a_y
            xs_name = "bx0" if p == 0 else "bx1"
            mu, musq, var, rstd, tmp = stats[p]
            if x_src is not None:
                kb.wait_guard("sync", xs_name)
                if f"w:{self._nm(x_src)}:q{tt // 2}" in kb.guard:
                    kb.wait_guard("sync", f"w:{self._nm(x_src)}:q{tt // 2}")
                else:
                    kb.wait_guard("sync", f"w:{self._nm(x_src)}")
                lx = kb.dma("sync", xs[:, 0:T],
                            xsv[:, :, tt * 512:(tt + 1) * 512], "ld")
                kb.set_guard(f"r:{self._nm(x_src)}", "ld", lx)
            if sub_src is not None:
                kb.wait_guard("sync", f"a_q{p}")
                if f"w:{self._nm(sub_src)}:q{tt // 2}" in kb.guard:
                    kb.wait_guard("sync", f"w:{self._nm(sub_src)}:q{tt // 2}")
                else:
                    kb.wait_guard("sync", f"w:{self._nm(sub_src)}")
                ls = kb.dma("sync", self.a_q[p][:, 0:T],
                            subv[:, :, tt * 512:(tt + 1) * 512], "ld")
                kb.set_guard(f"r:{self._nm(sub_src)}", "ld", ls)
            res_mil = None
            if sub_src is not None and not src_is_bf:
                kb.wait("vector", "ld", lx)
                kb.wait("vector", "ld", ls)
                for b in range(PB):
                    res_mil = kb.op("vector", lambda e, b=b, xs=xs, p=p:
                                    e.scalar_tensor_tensor(
                        xs[:, b * 512:(b + 1) * 512],
                        self.a_q[p][:, b * 512:(b + 1) * 512], coef,
                        xs[:, b * 512:(b + 1) * 512],
                        ALU.mult, ALU.add), inc=("v", 1) if b == PB - 1 else None)
                kb.set_guard(f"a_q{p}", "v", res_mil)
            elif src_is_bf:
                kb.wait("scalar", "ld", ls)
            else:
                kb.wait("vector", "ld", lx)
            if x_dst is not None:
                kb.wait("scalar", "v", res_mil)
                kb.wait_guard("scalar", f"r:{self._nm(x_dst)}")
                stx = kb.dma("scalar", xdv[:, :, tt * 512:(tt + 1) * 512],
                             xs[:, 0:T], "st")
                kb.set_guard(f"w:{self._nm(x_dst)}", "st", stx)
                kb.add_guard(xs_name, "st", stx)
            if ln_dst is None:
                if x_src is not None:
                    kb.add_guard(xs_name, "v", res_mil)
                continue
            if src_is_bf:
                xbf = self.a_q[p]
            else:
                xbf = self.a_k[p]
                kb.wait("scalar", "v", res_mil)
                kb.wait("scalar", "ld", lx)
                kb.wait_guard("scalar", f"a_k{p}")
                for b in range(PB):
                    kb.op("scalar", lambda e, b=b, xs=xs, xbf=xbf: e.activation(
                        xbf[:, b * 512:(b + 1) * 512],
                        xs[:, b * 512:(b + 1) * 512], AF.Copy))
            kb.wait_guard("scalar", f"a_v{p}")
            sm_sq = None
            for b in range(PB):
                sm_sq = kb.op("scalar", lambda e, b=b, p=p, xbf=xbf: e.activation(
                    self.a_v[p][:, b * 512:(b + 1) * 512],
                    xbf[:, b * 512:(b + 1) * 512], AF.Square),
                    inc=("s", 1) if b == PB - 1 else None)
            kb.wait("tensor", "s", sm_sq)
            kb.wait_guard("tensor", pp_name[p][0])
            kb.wait_guard("tensor", pp_name[p][1])
            pp = pp_ap[p]
            for k in range(PB):
                kb.op("tensor", lambda e, k=k, pp=pp, xbf=xbf: e.matmul(
                    pp[:, 0:512], self.oneD_bf[:, :],
                    xbf[:, k * 512:(k + 1) * 512],
                    start=(k == 0), stop=(k == PB - 1), skip_group_check=True))
            tm = None
            for k in range(PB):
                tm = kb.op("tensor", lambda e, k=k, pp=pp, p=p: e.matmul(
                    pp[:, 512:1024], self.oneD_bf[:, :],
                    self.a_v[p][:, k * 512:(k + 1) * 512],
                    start=(k == 0), stop=(k == PB - 1), skip_group_check=True),
                    inc=("t", 1) if k == PB - 1 else None)
            kb.set_guard(f"a_v{p}", "t", tm)
            if src_is_bf:
                kb.set_guard(f"a_q{p}", "t", tm)
            else:
                kb.set_guard(f"a_k{p}", "t", tm)
            kb.wait("vector", "t", tm)
            kb.wait_guard("vector", f"a_fvar{p}")
            if p == 1:
                kb.wait_guard("vector", "a_ctx")
                kb.wait_guard("vector", "a_st0")
            kb.op("vector", lambda e, mu=mu, pp=pp: e.tensor_copy(mu, pp[:, 0:512]))
            kb.op("vector", lambda e, mu=mu, musq=musq: e.tensor_tensor(
                musq, mu, mu, ALU.mult))
            vm1 = kb.op("vector", lambda e, var=var, musq=musq, pp=pp: e.tensor_tensor(
                var, pp[:, 512:1024], musq, ALU.subtract), inc=("v", 1))
            kb.set_guard(pp_name[p][0], "v", vm1)
            kb.set_guard(pp_name[p][1], "v", vm1)
            kb.wait("scalar", "v", vm1)
            kb.op("scalar", lambda e, var=var, rstd=rstd, eps_ap=eps_ap: e.activation(
                rstd, var, AF.Ln, bias=eps_ap))
            sm2 = kb.op("scalar", lambda e, rstd=rstd: e.activation(
                rstd, rstd, AF.Exp, scale=-0.5), inc=("s", 1))
            kb.add_guard(f"a_fvar{p}", "s", sm2)
            kb.wait("vector", "s", sm2)
            kb.wait_guard("vector", f"a_ebH{p}")
            vm2 = None
            for b in range(PB):
                src_b = (xbf if src_is_bf else xs)[:, b * 512:(b + 1) * 512]
                kb.op("vector", lambda e, src_b=src_b, tmp=tmp, mu=mu: e.tensor_tensor(
                    tmp, src_b, mu, ALU.subtract))
                if affine:
                    kb.op("vector", lambda e, tmp=tmp, rstd=rstd: e.tensor_tensor(
                        tmp, tmp, rstd, ALU.mult))
                    vm2 = kb.op("vector", lambda e, b=b, p=p, tmp=tmp: e.tensor_scalar(
                        self.a_e[:, p * 8192 + b * 512:p * 8192 + (b + 1) * 512],
                        tmp, self.c_gb[:, b:b + 1], self.c_gb[:, 8 + b:9 + b],
                        ALU.mult, ALU.add), inc=("v", 1) if b == PB - 1 else None)
                else:
                    vm2 = kb.op("vector", lambda e, b=b, p=p, tmp=tmp, rstd=rstd:
                                e.tensor_tensor(
                        self.a_e[:, p * 8192 + b * 512:p * 8192 + (b + 1) * 512],
                        tmp, rstd, ALU.mult), inc=("v", 1) if b == PB - 1 else None)
            kb.set_guard(f"a_fvar{p}", "v", vm2)
            if p == 1:
                kb.set_guard("a_ctx", "v", vm2)
                kb.set_guard("a_st0", "v", vm2)
            if x_src is not None:
                kb.set_guard(xs_name, "v", vm2)
                if x_dst is not None:
                    kb.add_guard(xs_name, "st", stx)
            if src_is_bf:
                kb.add_guard(f"a_q{p}", "v", vm2)
            kb.wait("scalar", "v", vm2)
            kb.wait_guard("scalar", f"r:{self._nm(ln_dst)}")
            stl = kb.dma("scalar", lnv[:, :, tt * 512:(tt + 1) * 512],
                         self.a_e[:, p * 8192:p * 8192 + 4096], "st")
            kb.set_guard(f"w:{self._nm(ln_dst)}", "st", stl)
            if tt % 2 == 1:
                kb.set_guard(f"w:{self._nm(ln_dst)}:q{tt // 2}", "st", stl)
            kb.set_guard(f"a_ebH{p}", "st", stl)
        if ln_dst is not None:
            kb.set_guard("a_e", "st", kb.cnt["st"])
            kb.wait_guard("sync", "a_ebH0")
            kb.wait_guard("sync", "a_ebH1")


    def _nm(self, ap):
        return ap.tensor.name if hasattr(ap, "tensor") else str(id(ap))

    def rope_g(self, buf):
        kb = self.kb
        bufv = self.fm(buf)
        kb.wait_guard("scalar", "a_e")
        lc = kb.dma("scalar", self.a_e[:, 0:T], self.cosbf, "st")
        lc = kb.dma("scalar", self.a_e[:, T:2 * T], self.sinbf, "st")
        for pb in range(PB):
            kb.wait_guard("scalar", "a_q0")
            kb.wait_guard("scalar", f"w:{self._nm(buf)}")
            lq = kb.dma("scalar", self.a_q[0][:, 0:T], bufv[:, pb, :], "st")
            vm_all = None
            for tt in range(NT):
                hp = tt % 2
                kb.wait_guard("tensor", f"p_ab{hp}")
                if tt == 0:
                    kb.wait("tensor", "st", lq)
                tm = kb.op("tensor", lambda e, tt=tt, hp=hp: e.matmul(
                    self.p_ab[:, hp * 512:(hp + 1) * 512],
                    self.c_p2[:, :], self.a_q[0][:, tt * 512:(tt + 1) * 512],
                    start=True, stop=True), inc=("t", 1))
                kb.wait("vector", "t", tm)
                kb.wait("vector", "st", lc)
                kb.wait_guard("vector", "a_ctx")
                kb.op("vector", lambda e, tt=tt: e.tensor_tensor(
                    self.a_f[0][:, :], self.a_q[0][:, tt * 512:(tt + 1) * 512],
                    self.a_e[:, tt * 512:(tt + 1) * 512], ALU.mult))
                kb.op("vector", lambda e, tt=tt, hp=hp: e.tensor_tensor(
                    self.a_f[1][:, :], self.p_ab[:, hp * 512:(hp + 1) * 512],
                    self.a_e[:, T + tt * 512:T + (tt + 1) * 512], ALU.mult))
                vm = kb.op("vector", lambda e, tt=tt: e.tensor_tensor(
                    self.a_ctx[:, tt * 512:(tt + 1) * 512],
                    self.a_f[0][:, :], self.a_f[1][:, :], ALU.add), inc=("v", 1))
                kb.set_guard(f"p_ab{hp}", "v", vm)
                vm_all = vm
            kb.wait("scalar", "v", vm_all)
            st = kb.dma("scalar", bufv[:, pb, :], self.a_ctx[:, 0:T], "st")
            kb.set_guard("a_ctx", "st", st)
            kb.set_guard("a_q0", "t", tm)
            kb.add_guard("a_q0", "v", vm_all)
            kb.set_guard(f"w:{self._nm(buf)}", "st", st)
            yield
        kb.set_guard("a_e", "v", self.kb.cnt["v"])

    def ffn_node(self, c, inputs_buf):
        kb = self.kb
        E = EDGES[c]
        def nsrc(i):
            return inputs_buf if i == -1 else self.nodes[i]
        wi = 0
        if c == 0:
            plist = [(E["eq"], E["in_q"], self.pq_buf)]
            pq, pk, pv = self.pq_buf, self.pq_buf, self.pq_buf
        elif c == 1:
            plist = [(E["eq"], E["in_q"], self.pq_buf), (E["ek"], E["in_k"], self.pk_buf)]
            pq, pk, pv = self.pq_buf, self.pk_buf, self.pq_buf
        else:
            plist = [(E["eq"], E["in_q"], self.pq_buf), (E["ek"], E["in_k"], self.pk_buf),
                     (E["ev"], E["in_v"], self.pv_buf)]
            pq, pk, pv = self.pq_buf, self.pk_buf, self.pv_buf
        for eidx, srcn, dstb in plist:
            self.proj(nsrc(srcn), self.We[eidx], dstb, wi=wi, be_idx=eidx)
            wi ^= 1
        self._ck(f"n{c}e")
        self.proj(pq, self.Wq[c], self.qh_buf, wi=wi, scale_ap=self.ew_ap(c, 0))
        wi ^= 1
        self.proj(pk, self.Wk[c], self.kh_buf, wi=wi, scale_ap=self.ew_ap(c, 1))
        wi ^= 1
        self.proj_tm(pv, self.Wv[c], wi=wi, scale_ap=self.ew_ap(c, 2))
        wi ^= 1
        self._ck(f"n{c}p")
        self.attn_all(H_IN, DH_IN, self.qh_buf, self.kh_buf, self.ctx_buf, RSQ_IN)
        self._ck(f"n{c}a")
        pqv = self.fm(self.pq_buf)
        aux_ld = {}
        def aux_load(q):
            kb.wait_guard("sync", "a_e")
            kb.wait("sync", "v", kb.cnt["v"])
            kb.wait_guard("sync", f"w:{self._nm(self.pq_buf)}")
            half = q % 2
            aux_ld[q] = kb.dma("sync",
                               self.a_e[:, half * 8192:(half + 1) * 8192]
                               .rearrange("p (k n) -> p k n", k=PB),
                               pqv[:, :, q * 1024:(q + 1) * 1024], "ld")
            kb.set_guard(f"r:{self._nm(self.pq_buf)}", "ld", aux_ld[q])
        ewq = self.ew_ap(c, 0)
        awc = self.aw_ap(c)
        def evac(kb_, mb, tl_abs, pslice, stage):
            q, tl = tl_abs // 2, tl_abs % 2
            base = (q % 2) * 8192
            kb.wait("vector", "ld", aux_ld[q])
            kb.op("vector", lambda e, mb=mb, tl=tl, base=base: e.tensor_scalar(
                self.a_f[2][:, :],
                self.a_e[:, base + mb * 1024 + tl * 512:base + mb * 1024 + tl * 512 + 512],
                ewq, None, ALU.mult))
            return kb.op("vector", lambda e, pslice=pslice, stage=stage: e.scalar_tensor_tensor(
                stage, pslice, awc, self.a_f[2][:, :], ALU.mult, ALU.add), inc=("v", 1))
        dst = self.nodes[c]
        self.proj(self.ctx_buf, self.Wo[c], dst, wi=wi, evac=evac, aux_load=aux_load)
        kb.set_guard("a_e", "v", kb.cnt["v"])

    def ffn(self, inputs_buf):
        for c in range(6):
            self.ffn_node(c, inputs_buf)
            self._ck(f"n{c}")
        self.boundary(None, self.nodes[5], None, None, self.sub_out, 1e-6,
                      affine=True, src_is_bf=True)

    def outer_attn(self):
        self.proj(self.ln_buf, self.wqkv[:, 0:D], self.qh_buf, wi=0)
        self.proj(self.ln_buf, self.wqkv[:, D:2 * D], self.kh_buf, wi=1)

        def ropes():
            yield from self.rope_g(self.qh_buf)
            yield from self.rope_g(self.kh_buf)
        self.proj_tm(self.ln_buf, self.wqkv[:, 2 * D:3 * D], wi=0,
                     scale_ap=None, interleave=ropes())
        self.attn_all(H_OUT, DH_OUT, self.qh_buf, self.kh_buf, self.ctx_buf, RSQ_OUT)
        self.proj(self.ctx_buf, self.w_o, self.sub_out, wi=1)

    def _ck(self, name):
        if self._stop == name:
            raise _StopBuild()

    def build(self, stop=None):
        kb = self.kb
        self._stop = stop
        try:
            self.emit_consts()
            self._ck("consts")
            self.boundary(self.x_in, None, None, None, self.ln_buf, 1e-5)
            self._ck("ln1")
            self.ffn(self.ln_buf)
            self._ck("ffnA")
            self.boundary(self.x_in, self.sub_out, 0.5, self.x_res, self.ln_buf, 1e-5)
            self._ck("bA")
            self.outer_attn()
            self._ck("outer")
            self.boundary(self.x_res, self.sub_out, 1.0, self.x_res, self.ln_buf, 1e-5)
            self.ffn(self.ln_buf)
            self._ck("ffnB")
            self.boundary(self.x_res, self.sub_out, 0.5, self.y_out, None, None)
        except _StopBuild:
            pass
        for sem in ("ld", "st", "wg", "t", "v", "s"):
            kb.wait("sync", sem, kb.cnt[sem])
        kb.finalize_checks()


class _StopBuild(Exception):
    pass


def build_program(stop=None, dbg=None):
    nc = bass.Bass("TRN2", target_bir_lowering=False)
    st = ExitStack()
    with st:
        prog = Prog(st, nc)
        if dbg is not None:
            dbg_out = nc.dram_tensor("dbg", [D, T], BF16, kind="ExternalOutput").ap()
        prog.build(stop=stop)
        if dbg is not None:
            kb = prog.kb
            if dbg.startswith("node"):
                src = prog.nodes[int(dbg[4:])]
            else:
                src = getattr(prog, dbg)
            if dbg == "vh_buf":
                src = src.rearrange("j p d -> (j p) d")
            kb.wait("gpsimd", "st", kb.cnt["st"])
            kb.wait("gpsimd", "ld", kb.cnt["ld"])
            fin = kb.dma("gpsimd", dbg_out, src, "wg")
            kb.wait("gpsimd", "wg", fin)
        with nc.Block() as block:
            prog.kb.emit_all(block)
    return nc


def _rope_tables():
    inv_freq = 1.0 / (10000.0 ** (np.arange(0, DH_OUT, 2, dtype=np.float64) / DH_OUT))
    t = np.arange(T, dtype=np.float64)
    freqs = np.outer(t, inv_freq)
    emb = np.concatenate([freqs, freqs], axis=-1)
    cos = np.cos(emb).astype(np.float32)
    sin = np.sin(emb).astype(np.float32)
    cosfm = np.ascontiguousarray(np.tile(cos.T, (2, 1))).astype(ml_dtypes.bfloat16)
    sinfm = np.ascontiguousarray(np.tile(sin.T, (2, 1))).astype(ml_dtypes.bfloat16)
    return cosfm, sinfm


def _p2_const():
    P = np.zeros((64, 64), dtype=np.float32)
    for dd in range(32):
        P[dd, 32 + dd] = -1.0
        P[32 + dd, dd] = 1.0
    blk = np.zeros((128, 128), dtype=np.float32)
    blk[0:64, 0:64] = P
    blk[64:128, 64:128] = P
    return np.ascontiguousarray(blk.T).astype(ml_dtypes.bfloat16)


def make_in_maps(L0_attn_x, L0_attn_w_qkv, L0_attn_w_o, L0_ffn_ew, L0_ffn_act_w,
                 L0_ffn_We, L0_ffn_be, L0_ffn_Wq, L0_ffn_Wk, L0_ffn_Wv, L0_ffn_Wo,
                 L0_ffn_ln_g, L0_ffn_ln_b):
    x = np.asarray(L0_attn_x, dtype=np.float32)
    wqkv = np.ascontiguousarray(np.asarray(L0_attn_w_qkv, dtype=np.float32))
    w_o = np.ascontiguousarray(np.asarray(L0_attn_w_o, dtype=np.float32))
    We = np.ascontiguousarray(np.asarray(L0_ffn_We, dtype=np.float32))
    Wq = np.ascontiguousarray(np.asarray(L0_ffn_Wq, dtype=np.float32))
    Wk = np.ascontiguousarray(np.asarray(L0_ffn_Wk, dtype=np.float32))
    Wv = np.ascontiguousarray(np.asarray(L0_ffn_Wv, dtype=np.float32))
    Wo = np.ascontiguousarray(np.asarray(L0_ffn_Wo, dtype=np.float32))
    be = np.asarray(L0_ffn_be, dtype=np.float32)
    ew = np.asarray(L0_ffn_ew, dtype=np.float32)
    act_w = np.asarray(L0_ffn_act_w, dtype=np.float32)
    ln_g = np.asarray(L0_ffn_ln_g, dtype=np.float32)
    ln_b = np.asarray(L0_ffn_ln_b, dtype=np.float32)

    befm = np.ascontiguousarray(be.reshape(20, 8, 128).transpose(2, 0, 1).reshape(128, 160))
    gfm = ln_g.reshape(8, 128).T
    bfm = ln_b.reshape(8, 128).T
    gbfm = np.ascontiguousarray(np.concatenate([gfm, bfm], axis=1))
    sca = np.concatenate([ew.reshape(-1), act_w.reshape(-1)]).reshape(1, 24)
    sca = np.ascontiguousarray(sca)
    cosfm, sinfm = _rope_tables()
    p2 = _p2_const()

    in_maps = []
    for b in range(x.shape[0]):
        in_maps.append(dict(
            x=np.ascontiguousarray(x[b].T),
            wqkv=wqkv, w_o=w_o, We=We, Wqm=Wq, Wkm=Wk, Wvm=Wv, Wom=Wo,
            befm=befm, gbfm=gbfm, sca=sca, cosbf=cosfm, sinbf=sinfm, p2=p2,
        ))
    return in_maps


def kernel(**inputs):
    in_maps = make_in_maps(**inputs)
    nc = build_program()
    res = run_bass_kernel_spmd(nc, in_maps, core_ids=list(range(8)))
    out = np.stack([np.asarray(r["y"]).T for r in res.results], axis=0)
    return np.ascontiguousarray(out)
